# version 06
# baseline (speedup 1.0000x reference)
import contextlib
import numpy as np
import ml_dtypes
import concourse.bass as bass
import concourse.mybir as mybir
from concourse.bass_utils import run_bass_kernel_spmd

F32 = mybir.dt.float32
BF16 = mybir.dt.bfloat16
I32 = mybir.dt.int32
AF = mybir.ActivationFunctionType
OP = mybir.AluOpType
AX = mybir.AxisListType
BFNP = ml_dtypes.bfloat16

D = 2048
DT = D // 128
NFF_A = 8192
NFF_Q = 16384
CH = 64
T1 = 512
T2 = 512
QW_W = 512


class Resrc:
    __slots__ = ("name", "w", "rs")

    def __init__(self, name):
        self.name = name
        self.w = None
        self.rs = []


class Prog:
    COMPUTE = ("pe", "dv", "act", "gp")

    def __init__(self):
        self.plans = {"pe": [], "dv": [], "act": [], "gp": [], "sp": []}
        self.cnt = {}
        self.vc = {k: {} for k in self.plans}
        self.semkeys = []
        self.rotc = {}

    def sem(self, key):
        if key not in self.cnt:
            self.cnt[key] = 0
            self.semkeys.append(key)
        return key

    def _wait(self, ek, semkey, val):
        if val <= 0 or self.vc[ek].get(semkey, 0) >= val:
            return
        self.vc[ek][semkey] = val
        self.plans[ek].append(("wait", semkey, val))

    def _edges(self, ek, reads, writes, war=()):
        need = {}
        for r in reads:
            if r.w is not None:
                k, v = r.w
                need[k] = max(need.get(k, 0), v)
        for w in list(writes) + list(war):
            if w.w is not None:
                k, v = w.w
                need[k] = max(need.get(k, 0), v)
            for (k, v) in w.rs:
                need[k] = max(need.get(k, 0), v)
        for k, v in need.items():
            self._wait(ek, k, v)

    def emit(self, ek, fn, reads=(), writes=(), semkey=None, inc=1, war=()):
        assert ek in self.COMPUTE
        semkey = self.sem(semkey or ek)
        self._edges(ek, reads, writes, war)
        self.cnt[semkey] += inc
        ev = (semkey, self.cnt[semkey])
        self.plans[ek].append(("op", fn, semkey, inc))
        for r in reads:
            r.rs.append(ev)
        for w in writes:
            w.w = ev
            w.rs = []
        return ev

    def dma_r(self, ek, semkey, rot, fn, reads=(), writes=(), war=()):
        return self.dma(ek, semkey, fn, reads=reads, writes=writes, rot=rot,
                        war=war)

    def dma(self, ek, semkey, fn, reads=(), writes=(), rot=1, war=()):
        if rot > 1:
            i = self.rotc.get(semkey, 0)
            self.rotc[semkey] = i + 1
            semkey = f"{semkey}{i % rot}"
        semkey = self.sem(semkey)
        self._edges(ek, reads, writes, war)
        self._wait(ek, semkey, self.cnt[semkey])
        self.cnt[semkey] += 16
        ev = (semkey, self.cnt[semkey])
        self.plans[ek].append(("op", fn, semkey, 16))
        for r in reads:
            r.rs.append(ev)
        for w in writes:
            w.w = ev
            w.rs = []
        return ev

    def wait_all(self, ek):
        for k in list(self.semkeys):
            self._wait(ek, k, self.cnt[k])

    def replay(self, nc):
        with contextlib.ExitStack() as stack:
            sems = {k: stack.enter_context(nc.semaphore(f"S_{k}"))
                    for k in self.semkeys}
            block = stack.enter_context(nc.Block())
            for ek, attr in (("sp", "sync"), ("pe", "tensor"),
                             ("dv", "vector"), ("act", "scalar"),
                             ("gp", "gpsimd")):
                plan = self.plans[ek]

                def body(eng, plan=plan):
                    for item in plan:
                        if item[0] == "wait":
                            eng.wait_ge(sems[item[1]], item[2])
                        else:
                            _, fn, semkey, inc = item
                            instr = fn(eng)
                            assert instr is not None
                            instr.then_inc(sems[semkey], inc)

                getattr(block, attr)(body)


def build_program(S4):
    assert S4 % T1 == 0 and S4 % T2 == 0 and T1 % CH == 0 and T2 % CH == 0
    NC1 = S4 // T1
    NC2 = S4 // T2
    CHN1 = T1 // CH
    CHN2 = T2 // CH
    nc = bass.Bass("TRN2", target_bir_lowering=False, debug=False,
                   disable_frame_to_traceback=True)
    p = Prog()
    res = {}

    def R(name):
        if name not in res:
            res[name] = Resrc(name)
        return res[name]

    def din(name, shape, dt=F32):
        return nc.dram_tensor(name, shape, dt, kind="ExternalInput").ap()

    x_d = din("x", [S4, D])
    wa_d = din("w_a", [D, D])
    wi_d = din("w_i", [D, D])
    wv_d = din("w_v", [D, D])
    op_d = din("out_proj_w", [D, D])
    wgu_d = din("w_gate_up", [D, 2 * NFF_A])
    wd_d = din("w_down", [NFF_A, D])
    gW_d = din("gate_W", [NFF_Q, D // 2], I32)
    gS_d = din("gate_S", [NFF_Q, D // 32])
    uW_d = din("up_W", [NFF_Q, D // 2], I32)
    uS_d = din("up_S", [NFF_Q, D // 32])
    dW_d = din("down_W", [D, NFF_Q // 2], I32)
    dS_d = din("down_S", [D, NFF_Q // 32])
    dbias_d = din("dbias", [128, DT])
    wpre_d = din("wpre", [128, DT])
    wffn_d = din("wffn", [128, DT])
    ident_d = din("ident", [128, 128])
    mask_d = din("mask", [128, 4])
    y_d = nc.dram_tensor("y", [S4, D], F32, kind="ExternalOutput").ap()

    def scr(name, shape, dt):
        return nc.dram_tensor(name, shape, dt, kind="Internal").ap()

    wa_blk = scr("wa_blk", [DT, 128, DT, 128], BF16)
    wi_blk = scr("wi_blk", [DT, 128, DT, 128], BF16)
    wv_blk = scr("wv_blk", [DT, 128, DT, 128], BF16)
    op_blk = scr("op_blk", [DT, 128, DT, 128], BF16)
    wgu_blk = scr("wgu_blk", [32, 128, DT, 512], BF16)
    wd_bf = scr("wd_bf", [NFF_A, D], BF16)
    dqgT = scr("dqgT", [32, 128, DT, 512], BF16)
    dquT = scr("dquT", [32, 128, DT, 512], BF16)
    dqdT = scr("dqdT", [128, 128, D], BF16)
    cd_scr = scr("cd_scr", [DT, 128, S4], F32)
    in_scr = scr("in_scr", [DT, 128, S4], F32)
    x0_scr = scr("x0_scr", [DT, 128, S4], F32)
    x1_scr = scr("x1_scr", [DT, 128, S4], F32)
    h2_scr = scr("h2_scr", [DT, 128, S4], BF16)
    c1r_scr = scr("c1r_scr", [1, S4], F32)
    m1r_scr = scr("m1r_scr", [1, S4], F32)
    cc_in = scr("cc_in", [128, 32], F32)
    cc_out = scr("cc_out", [512, 32], F32)

    cdv = cd_scr.rearrange("a p s -> p a s")
    inv = in_scr.rearrange("a p s -> p a s")
    x0v = x0_scr.rearrange("a p s -> p a s")
    x1v = x1_scr.rearrange("a p s -> p a s")
    h2v = h2_scr.rearrange("a p s -> p a s")

    stack = contextlib.ExitStack()

    def sb(name, shape, dt):
        return stack.enter_context(nc.sbuf_tensor(name, shape, dt))

    ident = sb("identS", [128, 128], F32)
    ones1 = sb("ones1", [1, 128], F32)
    onesb = sb("onesb", [128, 1], BF16)
    dbias = sb("dbiasS", [128, DT], F32)
    wpre = sb("wpreS", [128, DT], F32)
    wffn = sb("wffnS", [128, DT], F32)
    maskT = sb("maskT", [128, 4], F32)
    carry = sb("carry", [128, DT], F32)
    ctmp = sb("ctmp", [128, DT], F32)
    c0cur = sb("c0cur", [128, DT], F32)
    sumla = sb("sumla", [128, DT], F32)
    cfQ = sb("cfQ", [128, 32], F32)
    ccg = sb("ccg", [128, 4, 32], F32)
    cs8 = sb("cs8", [128, CHN2], F32)
    rows = sb("rows", [1, 3072], F32)
    XT = sb("XT", [128, 2, D], F32)
    XA = sb("XA", [128, DT, T2], F32)
    YA = sb("YA", [128, DT, T2], F32)
    CDB = sb("CDB", [128, DT, T2], F32)
    INB = sb("INB", [128, DT, T2], F32)
    HB = sb("HB", [128, DT, T2], BF16)
    IG = sb("IG", [128, 8, T2], BF16)
    IGf = IG[:].rearrange("p a b -> p (a b)").bitcast(F32)
    B2 = IGf[:, 0:T1]
    B3 = IGf[:, T1:2 * T1]
    B4 = IGf[:, 2 * T1:3 * T1]
    B2b = IGf[:, 3 * T1:4 * T1]
    B1t = sb("B1", [128, 2, T1], F32)
    B1 = B1t[:, 0, :]
    IGf = None
    B5 = sb("B5", [128, 16], F32)
    DMP = sb("DMP", [128, T2], BF16)
    identb = sb("identB", [128, 128], BF16)
    QWt = sb("QWt", [128, 2, QW_W], I32)
    QSC = sb("QSC", [128, 2, QW_W // 16], F32)
    QAB = sb("QAB", [128, QW_W], I32)
    QC = sb("QC", [128, 2, QW_W], I32)
    QD = sb("QD", [128, QW_W], F32)
    QF = sb("QF", [128, QW_W], F32)
    QO = sb("QOt", [128, 2, 2 * QW_W], BF16)
    QT = sb("QTt", [128, 1, 2 * QW_W], BF16)

    psum = stack.enter_context(nc.psum_tensor("ps", [128, 4096], F32))
    PSB = [R(f"psb{i}") for i in range(8)]

    def bank(i):
        return psum[:, 512 * i:512 * (i + 1)]

    CDW = CDB[:].rearrange("p a b -> p (a b)").bitcast(BF16).rearrange("p (a b) -> p a b", a=DT)
    INW = INB[:].rearrange("p a b -> p (a b)").bitcast(BF16).rearrange("p (a b) -> p a b", a=DT)
    INW8 = INB[:].rearrange("p a b -> p (a b)").bitcast(BF16).rearrange("p (a b) -> p a b", a=8)

    rXA, rYA, rCDB, rINB, rHB, rIG = (R("XA"), R("YA"), R("CDB"), R("INB"),
                                      R("HB"), R("IG"))

    class Ar:
        def __init__(self):
            self.cur = set()
            self.prev = []

    arCDB, arINB, arIG, arXA = Ar(), Ar(), Ar(), Ar()

    def aclaim(ar, r):
        if r in ar.cur:
            return []
        ar.cur.add(r)
        return ar.prev

    def aswitch(ar):
        if ar.cur:
            ar.prev = list(ar.cur)
            ar.cur = set()
    rXT = [R("XT0"), R("XT1")]
    rB1 = [R("B1_0"), R("B1_1")]
    rB = [rB1[0], R("B2"), R("B3"), R("B4"), R("B5")]
    rB2b = R("B2b")
    rDMP = R("DMP")
    XAW = (XA[:].rearrange("p a b -> p (a b)").bitcast(BF16)
           .rearrange("p (a b) -> p a b", a=DT))

    for t, d_src, rr in ((ident, ident_d, R("ident")),
                         (dbias, dbias_d, R("dbias")),
                         (wpre, wpre_d, R("wpre")), (wffn, wffn_d, R("wffn")),
                         (maskT, mask_d, R("maskT"))):
        p.dma_r("sp", "dxa", 2, lambda e, t=t, s=d_src: e.dma_start(t[:], s),
              writes=[rr])
    p.emit("act", lambda e: e.activation(identb[:], ident[:], AF.Copy),
           reads=[R("ident")], writes=[R("identb")])
    p.emit("dv", lambda e: e.memset(ones1[:], 1.0), writes=[R("ones1")])
    p.emit("dv", lambda e: e.memset(onesb[:], 1.0), writes=[R("onesb")])
    p.emit("dv", lambda e: e.memset(c0cur[:], 0.0), writes=[R("c0cur")])
    p.emit("dv", lambda e: e.memset(sumla[:], 0.0), writes=[R("sumla")])
    for cval in (1e-5, 1e-6):
        cten = sb(f"cst{cval:.0e}", [128, 1], F32)
        nc.const_aps.register(F32, cval, cten.ap())
        p.emit("dv", lambda e, cten=cten, cval=cval: e.memset(cten[:], cval),
               writes=[R("consts")])
    _initev = ("dv", p.cnt.get("dv", 0))
    for _ek in ("pe", "act", "gp", "sp"):
        p._wait(_ek, "dv", _initev[1])

    for i in range(4):
        sl = slice(i * (NFF_A // 4), (i + 1) * (NFF_A // 4))
        p.dma_r("gp", "gcv", 2,
                lambda e, sl=sl: e.dma_start(wd_bf[sl, :], wd_d[sl, :]),
                writes=[R("cv_wd")])
    CONV = CDW
    for nm, (w_src, blkt) in {"wa": (wa_d, wa_blk), "wi": (wi_d, wi_blk),
                              "wv": (wv_d, wv_blk), "op": (op_d, op_blk)}.items():
        src3f = w_src.rearrange("(a p) n -> p a n", p=128)
        for j in range(DT):
            s = j % 8
            rcv = R(f"cvb{s}")
            p.dma_r("gp", "gcv", 2,
                    lambda e, j=j, s=s, src3f=src3f: e.dma_start(
                        CONV[:, :, 128 * s:128 * (s + 1)],
                        src3f[:, :, j * 128:(j + 1) * 128]),
                    writes=[rcv], war=aclaim(arCDB, rcv))
            p.dma_r("sp", "gcs", 2,
                    lambda e, j=j, s=s, blkt=blkt: e.dma_start(
                        blkt[j].rearrange("p a n -> p (a n)")
                        .rearrange("p (a n) -> p a n", a=DT),
                        CONV[:, :, 128 * s:128 * (s + 1)]),
                    reads=[rcv], writes=[R(f"cv_{nm}")])
    aswitch(arCDB)
    wgu3f = wgu_d.rearrange("(a p) n -> p a n", p=128)
    for lg in range(32):
        col0 = lg * 512
        half = lg % 2
        rcv = R(f"cvg{half}")
        p.dma_r("gp", "gcv", 2,
                lambda e, col0=col0, half=half: e.dma_start(
                    CDW[:, :, 512 * half:512 * (half + 1)],
                    wgu3f[:, :, col0:col0 + 512]),
                writes=[rcv], war=aclaim(arCDB, rcv))
        p.dma_r("sp", "gcs", 2,
                lambda e, lg=lg, half=half: e.dma_start(
                    wgu_blk[lg], CDW[:, :, 512 * half:512 * (half + 1)]),
                reads=[rcv], writes=[R("cv_wgu")])

    dq_queue = []
    rdq = {"g": R("dq_g"), "u": R("dq_u"), "d": R("dq_d")}
    psumb = psum[:].rearrange("p (a b) -> p a b", a=8).bitcast(BF16)

    def dequant_tile(W_src, S_src, dst_fn, r0, c0, slot, rtag):
        qw = QWt[:, slot, :]
        qsc = QSC[:, slot, :]
        qo = QO[:, slot, :]
        qt = QT[:, 0, :]
        dqb = 6 + slot
        rqw, rqs = R(f"qw{slot}"), R(f"qs{slot}")
        rqo, rqt = R(f"qo{slot}"), R("qt0")
        a = b = QAB[:]
        d, f = QD[:], QF[:]
        ra = rb = R("q_ab")
        rd, rf = R("q_d"), R("q_f")

        def plan():
            p.dma_r("sp", "gdq", 2,
                  lambda e: e.dma_start(qw, W_src[r0:r0 + 128, c0:c0 + QW_W]),
                  writes=[rqw])
            p.dma_r("sp", "gdq", 2,
                  lambda e: e.dma_start(
                      qsc, S_src[r0:r0 + 128, c0 // 16:(c0 + QW_W) // 16]),
                  writes=[rqs])
            p.emit("dv", lambda e: e.tensor_copy(
                f.rearrange("p (k r) -> p k r", r=16),
                qsc.unsqueeze(2).broadcast_to([128, QW_W // 16, 16])),
                reads=[rqs], writes=[rf])
            for half in range(2):
                c = QC[:, slot, :]
                rc = R(f"qc{slot}")
                if half == 0:
                    p.emit("dv", lambda e: e.tensor_scalar(
                        a, qw, 7, None, OP.bitwise_and),
                        reads=[rqw], writes=[ra])
                else:
                    p.emit("dv", lambda e: e.tensor_scalar(
                        a, qw, 4, 7, OP.logical_shift_right, OP.bitwise_and),
                        reads=[rqw], writes=[ra])
                p.emit("gp", lambda e, c=c: e.tensor_scalar(
                    c, a, float(1 << 22), float(0x3F000000), OP.mult, OP.add),
                    reads=[ra], writes=[rc])
                p.emit("dv", lambda e: e.tensor_scalar(
                    d, a, -0.25, 0.5, OP.mult, OP.add),
                    reads=[ra], writes=[rd])
                p.emit("act", lambda e: e.activation(d, d, AF.Relu),
                       reads=[rd], writes=[rd])
                eng_s = "dv" if half == 0 else "gp"
                p.emit(eng_s, lambda e, c=c: e.tensor_tensor(
                    c.bitcast(F32), c.bitcast(F32), d, OP.subtract),
                    reads=[rc, rd], writes=[rc])
                if half == 0:
                    p.emit("dv", lambda e: e.tensor_scalar(
                        b, qw, 8, 28, OP.bitwise_and, OP.logical_shift_left),
                        reads=[rqw], writes=[rb])
                else:
                    p.emit("dv", lambda e: e.tensor_scalar(
                        b, qw, 128, 24, OP.bitwise_and, OP.logical_shift_left),
                        reads=[rqw], writes=[rb])
                p.emit("dv", lambda e, c=c: e.tensor_tensor(
                    c, c, b, OP.bitwise_or),
                    reads=[rc, rb], writes=[rc])
                qo_h = qo.rearrange("p (k two) -> p k two", two=2)[:, :, half]
                p.emit("dv", lambda e, c=c, qo_h=qo_h: e.tensor_tensor(
                    qo_h, c.bitcast(F32), f, OP.mult),
                    reads=[rc, rf], writes=[rqo])

            def tfn(e):
                mm = None
                for t in range(8):
                    mm = e.transpose(
                        psumb[:, dqb, 128 * t:128 * (t + 1)],
                        qo[:, 128 * t:128 * (t + 1)], identb[:])
                return mm
            p.emit("pe", tfn, reads=[rqo, R("identb")], writes=[PSB[dqb]])
            p.emit("act", lambda e: e.activation(qt, psumb[:, dqb, :],
                                                 AF.Copy),
                   reads=[PSB[dqb]], writes=[rqt])
            p.dma_r("act", "gds", 2,
                  lambda e: e.dma_start(dst_fn(), qt.rearrange(
                      "p (a n) -> p a n", a=8)),
                  reads=[rqt], writes=[rtag])
        dq_queue.append(plan)

    dqdTv = dqdT.rearrange("j p d -> p j d")
    sl_i = [0]
    for (Wsrc, Ssrc, dqT, nrows, words, key) in (
            (gW_d, gS_d, dqgT, NFF_Q, D // 2, "g"),
            (uW_d, uS_d, dquT, NFF_Q, D // 2, "u"),
            (dW_d, dS_d, None, D, NFF_Q // 2, "d")):
        for r0 in range(0, nrows, 128):
            for c0 in range(0, words, QW_W):
                if key == "d":
                    j0 = (2 * c0) // 128
                    dst_fn = (lambda j0=j0, r0=r0:
                              dqdTv[:, j0:j0 + 8, r0:r0 + 128])
                else:
                    lg, n0 = r0 // 512, r0 % 512
                    hc0 = (2 * c0) // 128
                    dst_fn = (lambda dqT=dqT, lg=lg, hc0=hc0, n0=n0:
                              dqT[lg, :, hc0:hc0 + 8, n0:n0 + 128])
                dequant_tile(Wsrc, Ssrc, dst_fn, r0, c0, sl_i[0] % 2,
                             rdq[key])
                sl_i[0] += 1
    DQ_TOTAL = len(dq_queue)
    dq_done = [0]

    def run_dq(n):
        while n > 0 and dq_done[0] < DQ_TOTAL:
            dq_queue[dq_done[0]]()
            dq_done[0] += 1
            n -= 1

    def mmgrp(out_ap, pairs, bank_res_list, extra_reads, first=True,
              last=True):
        def fn(e):
            mm = None
            n = len(pairs)
            for i, (lt, rt) in enumerate(pairs):
                mm = e.matmul(out_ap, lt, rt, start=(first and i == 0),
                              stop=(last and i == n - 1))
            return mm
        return p.emit("pe", fn, reads=list(extra_reads), writes=bank_res_list)

    def row(off, t):
        return rows[0:1, off:off + t]

    def ln_stats(arena, arena_res, t_w, kind, eps, off):
        want_mean = kind != "rms"
        rsumv = psum[0:1, 6 * 512:6 * 512 + t_w]
        rsqv = psum[0:1, 7 * 512:7 * 512 + t_w]
        for i in range(DT):
            p.emit("act", lambda e, i=i: e.activation(
                DMP[:, 0:t_w], arena[:, i, 0:t_w], AF.Square),
                reads=[arena_res], writes=[rDMP])
            mmgrp(rsqv, [(onesb[:, :], DMP[:, 0:t_w])], [PSB[7]],
                  [rDMP, R("onesb")], first=(i == 0), last=(i == DT - 1))
            if want_mean:
                p.emit("act", lambda e, i=i: e.activation(
                    DMP[:, 0:t_w], arena[:, i, 0:t_w], AF.Copy),
                    reads=[arena_res], writes=[rDMP])
                mmgrp(rsumv, [(onesb[:, :], DMP[:, 0:t_w])], [PSB[6]],
                      [rDMP, R("onesb")], first=(i == 0), last=(i == DT - 1))
        if kind == "rms":
            ro = row(off, t_w)
            rr = R(f"row{off}")
            p.emit("act", lambda e: e.activation(
                ro, rsqv, AF.Ln, bias=float(eps), scale=1.0 / D),
                reads=[PSB[7]], writes=[rr])
            p.emit("act", lambda e: e.activation(ro, ro, AF.Exp, scale=-0.5),
                   reads=[rr], writes=[rr])
            return
        mu = row(2048, t_w)
        var = row(2560, t_w)
        rmu, rvar = R("rowmu"), R("rowvar")
        p.emit("act", lambda e: e.activation(mu, rsumv, AF.Copy, scale=1.0 / D),
               reads=[PSB[6]], writes=[rmu])
        p.emit("act", lambda e: e.activation(var, mu, AF.Square),
               reads=[rmu], writes=[rvar])
        p.emit("dv", lambda e: e.scalar_tensor_tensor(
            var, rsqv, 1.0 / D, var, OP.mult, OP.subtract),
            reads=[PSB[7], rvar], writes=[rvar])
        co = row(off, t_w)
        mo = row(off + t_w, t_w)
        rco, rmo = R(f"row{off}"), R(f"row{off}m")
        if kind == "ln":
            p.emit("act", lambda e: e.activation(co, var, AF.Ln,
                                                 bias=float(eps)),
                   reads=[rvar], writes=[rco])
            p.emit("act", lambda e: e.activation(co, co, AF.Exp, scale=-0.5),
                   reads=[rco], writes=[rco])
        else:
            s = row(1024, t_w)
            t = row(2560, t_w)
            ms = row(1536, t_w)
            rs_, rt_, rms_ = R("rowS"), R("rowT"), R("rowMS")
            p.emit("act", lambda e: e.activation(s, var, AF.Ln, bias=1e-5),
                   reads=[rvar], writes=[rs_])
            p.emit("act", lambda e: e.activation(s, s, AF.Exp, scale=-0.5),
                   reads=[rs_], writes=[rs_])
            p.emit("dv", lambda e: e.tensor_tensor(ms, mu, s, OP.mult),
                   reads=[rmu, rs_], writes=[rms_])
            p.emit("dv", lambda e: e.tensor_tensor(t, var, s, OP.mult),
                   reads=[rvar, rs_], writes=[rt_])
            p.emit("dv", lambda e: e.tensor_tensor(t, t, s, OP.mult),
                   reads=[rt_, rs_], writes=[rt_])
            p.emit("act", lambda e: e.activation(t, t, AF.Ln, bias=1e-6),
                   reads=[rt_], writes=[rt_])
            p.emit("act", lambda e: e.activation(t, t, AF.Exp, scale=-0.5),
                   reads=[rt_], writes=[rt_])
            p.emit("dv", lambda e: e.tensor_tensor(co, s, t, OP.mult),
                   reads=[rs_, rt_], writes=[rco])
        p.emit("dv", lambda e: e.tensor_tensor(mo, mu, co, OP.mult),
               reads=[rmu, rco], writes=[rmo])

    def bcast(row_ap, row_res, bank_i, t_w):
        nb = (t_w + 511) // 512
        outs = [PSB[bank_i + i] for i in range(nb)]

        def fn(e):
            mm = None
            for i in range(nb):
                w = min(512, t_w - i * 512)
                mm = e.matmul(
                    psum[:, 512 * (bank_i + i):512 * (bank_i + i) + w],
                    ones1[:, :], row_ap[:, i * 512:i * 512 + w],
                    start=True, stop=True)
            return mm
        p.emit("pe", fn, reads=[row_res, R("ones1")], writes=outs)
        return psum[:, 512 * bank_i:512 * bank_i + t_w], outs


    def wslot(s):
        return CDW.rearrange("p a (s c) -> p a s c", c=128)[:, :, s, :]

    aswitch(arCDB)
    for ch1 in range(NC1):
        t0 = ch1 * T1
        for tb in range(T1 // 128):
            sl = slice(t0 + tb * 128, t0 + (tb + 1) * 128)
            slot = tb % 2
            p.dma_r("sp", "dxa", 2,
                  lambda e, sl=sl, slot=slot: e.dma_start(XT[:, slot, :],
                                                          x_d[sl, :]),
                  writes=[rXT[slot]])
            for bq in range(4):
                def tfn(e, slot=slot, bq=bq):
                    mm = None
                    for k in range(4):
                        dt_i = bq * 4 + k
                        mm = e.transpose(
                            psum[:, bq * 512 + 128 * k:bq * 512 + 128 * (k + 1)],
                            XT[:, slot, 128 * dt_i:128 * (dt_i + 1)], ident[:])
                    return mm
                p.emit("pe", tfn, reads=[rXT[slot], R("ident")],
                       writes=[PSB[bq]])
                p.emit("act", lambda e, bq=bq, tb=tb: e.activation(
                    XA[:, bq * 4:(bq + 1) * 4, tb * 128:(tb + 1) * 128],
                    psum[:, bq * 512:(bq + 1) * 512]
                    .rearrange("p (a b) -> p a b", a=4), AF.Copy),
                    reads=[PSB[bq]], writes=[rXA])
        p.dma_r("sp", "dsc", 4,
              lambda e, t0=t0: e.dma_start(x0v[:, :, t0:t0 + T1],
                                           XA[:, :, 0:T1]),
              reads=[rXA], writes=[R("x0s")])
        ln_stats(XA, rXA, T1, "ln1", None, 0)
        p.dma_r("sp", "dsc", 4, lambda e, t0=t0: e.dma_start(
            c1r_scr[0:1, t0:t0 + T1], row(1024, T1)),
            reads=[R("rowS")], writes=[R("c1rs")])
        p.dma_r("sp", "dsc", 4, lambda e, t0=t0: e.dma_start(
            m1r_scr[0:1, t0:t0 + T1], row(1536, T1)),
            reads=[R("rowMS")], writes=[R("m1rs")])
        cbc, cbr = bcast(row(0, T1), R("row0"), 6, T1)
        mbc, mbr = bcast(row(T1, T1), R("row0m"), 7, T1)
        for i in range(DT):
            p.emit("dv", lambda e, i=i: e.tensor_tensor(
                B1[:], XA[:, i, 0:T1], cbc, OP.mult),
                reads=[rXA] + cbr, writes=[rB[0]])
            p.emit("dv", lambda e: e.tensor_tensor(B1[:], B1[:], mbc,
                                                   OP.subtract),
                   reads=[rB[0]] + mbr, writes=[rB[0]])
            p.emit("dv", lambda e, i=i: e.tensor_scalar(
                HB[:, i, 0:T1], B1[:], wpre[:, i:i + 1], None, OP.mult),
                reads=[rB[0], R("wpre")], writes=[rHB])
        for j in range(DT):
            s0 = (j % 2) * 3
            rws = [R(f"ws{s0 + k}") for k in range(3)]
            for k, (wblk, rw, cv) in enumerate(
                    ((wa_blk, rws[0], "cv_wa"), (wi_blk, rws[1], "cv_wi"),
                     (wv_blk, rws[2], "cv_wv"))):
                p.dma_r("gp", "dwa", 4,
                      lambda e, wblk=wblk, j=j, s=s0 + k: e.dma_start(
                          wslot(s), wblk[j]),
                      reads=[R(cv)], writes=[rw], war=aclaim(arCDB, rw))
            ba, bi, bv = (j % 2), 2 + (j % 2), 4 + (j % 2)
            for (s, rw, bk) in ((s0, rws[0], ba), (s0 + 1, rws[1], bi),
                                (s0 + 2, rws[2], bv)):
                pairs = [(wslot(s)[:, kc, :], HB[:, kc, 0:T1])
                         for kc in range(DT)]
                mmgrp(bank(bk), pairs, [PSB[bk]], [rw, rHB])
            Bj = B1t[:, j % 2, :]
            rBj = rB1[j % 2]
            p.emit("act", lambda e, ba=ba, j=j, Bj=Bj: e.activation(
                Bj, bank(ba), AF.Sigmoid, bias=dbias[:, j:j + 1]),
                reads=[PSB[ba], R("dbias")], writes=[rBj])
            p.emit("act", lambda e, bi=bi: e.activation(
                B2, bank(bi), AF.Sigmoid),
                reads=[PSB[bi]], writes=[rB[1]], war=aclaim(arIG, rB[1]))
            p.emit("act", lambda e, Bj=Bj: e.activation(B3, Bj, AF.Square),
                   reads=[rBj], writes=[rB[2]], war=aclaim(arIG, rB[2]))
            p.emit("act", lambda e: e.activation(B3, B3, AF.Sqrt,
                                                 bias=1.0, scale=-1.0),
                   reads=[rB[2]], writes=[rB[2]])
            p.emit("dv", lambda e, bv=bv: e.tensor_tensor(
                B4, B2, bank(bv), OP.mult),
                reads=[rB[1], PSB[bv]], writes=[rB[3]],
                war=aclaim(arIG, rB[3]))
            p.emit("gp", lambda e: e.tensor_tensor(B4, B3, B4,
                                                   OP.mult),
                   reads=[rB[2], rB[3]], writes=[rB[3]])
            p.emit("act", lambda e, Bj=Bj: e.activation(B2b, Bj, AF.Ln),
                   reads=[rBj], writes=[rB2b], war=aclaim(arIG, rB2b))
            for cc in range(CHN1):
                sl = slice(cc * CH, (cc + 1) * CH)
                p.emit("dv", lambda e, sl=sl: e.tensor_tensor_scan(
                    B2b[:, sl], B2b[:, sl], B2b[:, sl], 0.0, OP.add,
                    OP.bypass),
                    reads=[rB2b], writes=[rB2b])
            p.emit("dv", lambda e: e.reduce_sum(B5[:, 0:1],
                                                B2b[:, CH - 1::CH], AX.X),
                   reads=[rB2b], writes=[rB[4]])
            p.emit("gp", lambda e, j=j: e.tensor_tensor(
                sumla[:, j:j + 1], sumla[:, j:j + 1], B5[:, 0:1], OP.add),
                reads=[rB[4]], writes=[R("sumla")])
            p.emit("act", lambda e, Bj=Bj: e.activation(Bj, B2b, AF.Exp),
                   reads=[rB2b], writes=[rBj])
            p.emit("dv", lambda e, Bj=Bj: e.tensor_scalar(
                B2b, Bj, 1e-10, None, OP.max),
                reads=[rBj], writes=[rB2b])
            p.emit("dv", lambda e: e.reciprocal(B2b, B2b),
                   reads=[rB2b], writes=[rB2b])
            p.emit("dv", lambda e: e.tensor_tensor(B4, B4, B2b, OP.mult),
                   reads=[rB[3], rB2b], writes=[rB[3]])
            for cc in range(CHN1):
                sl = slice(cc * CH, (cc + 1) * CH)
                p.emit("dv", lambda e, sl=sl: e.tensor_tensor_scan(
                    B4[:, sl], B4[:, sl], B4[:, sl], 0.0, OP.add, OP.bypass),
                    reads=[rB[3]], writes=[rB[3]])
            p.emit("dv", lambda e, Bj=Bj: e.tensor_tensor(B4, Bj, B4,
                                                          OP.mult),
                   reads=[rBj, rB[3]], writes=[rB[3]])
            p.emit("dv", lambda e, j=j, Bj=Bj: e.tensor_tensor_scan(
                B5[:, 1:1 + CHN1], Bj[:, CH - 1::CH], B4[:, CH - 1::CH],
                c0cur[:, j:j + 1], OP.mult, OP.add),
                reads=[rBj, rB[3], R("c0cur")], writes=[rB[4]])
            p.emit("gp", lambda e, j=j: e.tensor_copy(
                c0cur[:, j:j + 1], B5[:, CHN1:CHN1 + 1]),
                reads=[rB[4]], writes=[R("c0cur")])
            p.dma_r("sp", "dsc", 4,
                  lambda e, j=j, t0=t0, Bj=Bj: e.dma_start(
                      cdv[:, j, t0:t0 + T1], Bj),
                  reads=[rBj], writes=[R("cds")])
            p.dma_r("sp", "dsc", 4,
                  lambda e, j=j, t0=t0: e.dma_start(inv[:, j, t0:t0 + T1],
                                                    B4),
                  reads=[rB[3]], writes=[R("ins")])
            run_dq(3)

    p.emit("dv", lambda e: e.tensor_copy(cfQ[:, 0:DT], c0cur[:]),
           reads=[R("c0cur")], writes=[R("cfQ")])
    p.emit("act", lambda e: e.activation(cfQ[:, DT:2 * DT], sumla[:], AF.Exp),
           reads=[R("sumla")], writes=[R("cfQ")])
    p.dma("sp", "dcc", lambda e: e.dma_start(cc_in[:, :], cfQ[:]),
          reads=[R("cfQ")], writes=[R("ccin")])
    p.emit("gp", lambda e: e.collective_compute(
        "AllGather", OP.bypass, replica_groups=[[0, 1, 2, 3], [4, 5, 6, 7]],
        ins=[cc_in.opt()], outs=[cc_out.opt()]),
        reads=[R("ccin")], writes=[R("ccout")], semkey="ccs", inc=1)
    ccov = cc_out.rearrange("(r p) c -> p r c", p=128)
    p.dma("sp", "dcc", lambda e: e.dma_start(ccg[:], ccov),
          reads=[R("ccout")], writes=[R("ccg")])
    p.emit("dv", lambda e: e.memset(carry[:], 0.0), writes=[R("carry")])
    for r in range(4):
        p.emit("dv", lambda e, r=r: e.tensor_tensor(
            ctmp[:], carry[:], ccg[:, r, DT:2 * DT], OP.mult),
            reads=[R("carry"), R("ccg")], writes=[R("ctmp")])
        p.emit("dv", lambda e, r=r: e.tensor_tensor(
            ctmp[:], ctmp[:], ccg[:, r, 0:DT], OP.add),
            reads=[R("ctmp"), R("ccg")], writes=[R("ctmp")])
        p.emit("dv", lambda e: e.tensor_tensor(
            ctmp[:], ctmp[:], carry[:], OP.subtract),
            reads=[R("ctmp"), R("carry")], writes=[R("ctmp")])
        p.emit("dv", lambda e, r=r: e.scalar_tensor_tensor(
            carry[:], ctmp[:], maskT[:, r:r + 1], carry[:], OP.mult, OP.add),
            reads=[R("ctmp"), R("maskT")], writes=[R("carry")])

    def mlp(rhs_res, act_f, nff, gate_loader, down_loader, first_copy):
        aswitch(arCDB)
        aswitch(arINB)
        aswitch(arXA)
        aswitch(arIG)
        n_sg = nff // 1024
        for sg in range(n_sg):
            for lg in range(2):
                par = (sg * 2 + lg) % 2
                W = CDW if par == 0 else XAW
                ar = arCDB if par == 0 else arXA
                jj0 = sg * 8 + lg * 4
                rg, ru = R(f"wgt_g{par}"), R(f"wgt_u{par}")
                gate_loader(jj0, W, ar, rg, ru)
                for k in range(4):
                    jj = lg * 4 + k
                    bg, bu = (jj % 2), 2 + (jj % 2)
                    gp_ = [(W[:, kc, 128 * k:128 * (k + 1)], HB[:, kc, :])
                           for kc in range(DT)]
                    up_ = [(W[:, kc, 512 + 128 * k:512 + 128 * (k + 1)],
                            HB[:, kc, :]) for kc in range(DT)]
                    mmgrp(bank(bg), gp_, [PSB[bg]], [rg, rhs_res])
                    mmgrp(bank(bu), up_, [PSB[bu]], [ru, rhs_res])
                    p.emit("act", lambda e, bg=bg: e.activation(
                        B1[:, 0:T2], bank(bg), act_f),
                        reads=[PSB[bg]], writes=[rB[0]])
                    p.emit("dv", lambda e, bu=bu, jj=jj: e.tensor_tensor(
                        IG[:, jj, :], B1[:, 0:T2], bank(bu), OP.mult),
                        reads=[rB[0], PSB[bu]], writes=[rIG],
                        war=aclaim(arIG, rIG))
            rwd = [R(f"wd{s}") for s in range(8)]
            for s in range(8):
                down_loader(s, sg * 8 + s, rwd[s])
            for q in range(4):
                def dfn(e, q=q):
                    mm = None
                    for jj in range(8):
                        for k in range(4):
                            dcc = q * 4 + k
                            mm = e.matmul(
                                bank(4 + k),
                                INW8[:, jj, 128 * dcc:128 * (dcc + 1)],
                                IG[:, jj, :],
                                start=(jj == 0), stop=(jj == 7))
                    return mm
                p.emit("pe", dfn, reads=rwd + [rIG],
                       writes=[PSB[4 + i] for i in range(4)])
                dst = YA[:, q * 4:(q + 1) * 4, :]
                srcv = psum[:, 4 * 512:8 * 512].rearrange("p (a b) -> p a b",
                                                          a=4)
                if first_copy and sg == 0:
                    p.emit("dv", lambda e, dst=dst, srcv=srcv: e.tensor_copy(
                        dst, srcv),
                        reads=[PSB[4 + i] for i in range(4)], writes=[rYA])
                else:
                    p.emit("dv", lambda e, dst=dst, srcv=srcv: e.tensor_tensor(
                        dst, dst, srcv, OP.add),
                        reads=[PSB[4 + i] for i in range(4)], writes=[rYA])
            run_dq(6)

    def attn_gate_loader(jj0, W, ar, rg, ru):
        p.dma_r("gp", "dwg", 4, lambda e, jj0=jj0, W=W: e.dma_start(
            W[:, :, 0:512], wgu_blk[jj0 // 4]),
            reads=[R("cv_wgu")], writes=[rg], war=aclaim(ar, rg))
        p.dma_r("gp", "dwg", 4, lambda e, jj0=jj0, W=W: e.dma_start(
            W[:, :, 512:1024], wgu_blk[16 + jj0 // 4]),
            reads=[R("cv_wgu")], writes=[ru], war=aclaim(ar, ru))

    def attn_down_loader(s, jj_abs, rw):
        p.dma_r("sp", "dwg", 4, lambda e, s=s, jj_abs=jj_abs: e.dma_start(
            INW8[:, s, :], wd_bf[jj_abs * 128:(jj_abs + 1) * 128, :]),
            reads=[R("cv_wd")], writes=[rw], war=aclaim(arINB, rw))

    def q_gate_loader(jj0, W, ar, rg, ru):
        p.dma_r("gp", "dwg", 4, lambda e, jj0=jj0, W=W: e.dma_start(
            W[:, :, 0:512], dqgT[jj0 // 4]),
            reads=[rdq["g"]], writes=[rg], war=aclaim(ar, rg))
        p.dma_r("gp", "dwg", 4, lambda e, jj0=jj0, W=W: e.dma_start(
            W[:, :, 512:1024], dquT[jj0 // 4]),
            reads=[rdq["u"]], writes=[ru], war=aclaim(ar, ru))

    def q_down_loader(s, jj_abs, rw):
        p.dma_r("sp", "dwg", 4, lambda e, s=s, jj_abs=jj_abs: e.dma_start(
            INW8[:, s, :], dqdT[jj_abs]),
            reads=[rdq["d"]], writes=[rw], war=aclaim(arINB, rw))

    for ch2 in range(NC2):
        t0 = ch2 * T2
        aswitch(arCDB)
        aswitch(arINB)
        p.dma_r("sp", "dld", 4, lambda e, t0=t0: e.dma_start(
            CDB[:, :, :], cdv[:, :, t0:t0 + T2]),
            reads=[R("cds")], writes=[rCDB], war=aclaim(arCDB, rCDB))
        p.dma_r("sp", "dld", 4, lambda e, t0=t0: e.dma_start(
            INB[:, :, :], inv[:, :, t0:t0 + T2]),
            reads=[R("ins")], writes=[rINB], war=aclaim(arINB, rINB))
        p.dma_r("sp", "dld", 4, lambda e, t0=t0: e.dma_start(
            XA[:, :, :], x0v[:, :, t0:t0 + T2]),
            reads=[R("x0s")], writes=[rXA], war=aclaim(arXA, rXA))
        for j in range(DT):
            p.emit("dv", lambda e, j=j: e.tensor_tensor_scan(
                cs8[:, :], CDB[:, j, CH - 1::CH], INB[:, j, CH - 1::CH],
                carry[:, j:j + 1], OP.mult, OP.add),
                reads=[rCDB, rINB, R("carry")], writes=[R("cs8")])
            p.emit("dv", lambda e, j=j: e.scalar_tensor_tensor(
                HB[:, j, 0:CH], CDB[:, j, 0:CH], carry[:, j:j + 1],
                INB[:, j, 0:CH], OP.mult, OP.add),
                reads=[rCDB, rINB, R("carry")], writes=[rHB])
            p.emit("dv", lambda e, j=j: e.tensor_tensor(
                B1[:].rearrange("p (a b) -> p a b", a=CHN2)[:, 1:, :],
                CDB[:, j, CH:].rearrange("p (a b) -> p a b", a=CHN2 - 1),
                cs8[:, 0:CHN2 - 1].unsqueeze(2)
                .broadcast_to([128, CHN2 - 1, CH]), OP.mult),
                reads=[rCDB, R("cs8")], writes=[rB[0]])
            p.emit("dv", lambda e, j=j: e.tensor_tensor(
                HB[:, j, CH:], B1[:, CH:], INB[:, j, CH:], OP.add),
                reads=[rB[0], rINB], writes=[rHB])
            p.emit("dv", lambda e, j=j: e.tensor_copy(
                carry[:, j:j + 1], cs8[:, CHN2 - 1:CHN2]),
                reads=[R("cs8")], writes=[R("carry")])
        p.dma_r("sp", "dld", 4, lambda e, t0=t0: e.dma_start(
            row(0, T2), c1r_scr[0:1, t0:t0 + T2]),
            reads=[R("c1rs")], writes=[R("row0")])
        p.dma_r("sp", "dld", 4, lambda e, t0=t0: e.dma_start(
            row(512, T2), m1r_scr[0:1, t0:t0 + T2]),
            reads=[R("m1rs")], writes=[R("row0m")])
        sbc, sbr = bcast(row(0, T2), R("row0"), 6, T2)
        msbc, msbr = bcast(row(512, T2), R("row0m"), 7, T2)
        for i in range(DT):
            p.emit("dv", lambda e, i=i: e.tensor_tensor(
                YA[:, i, :], XA[:, i, :], sbc, OP.mult),
                reads=[rXA] + sbr, writes=[rYA])
            p.emit("dv", lambda e, i=i: e.tensor_tensor(
                YA[:, i, :], YA[:, i, :], msbc, OP.subtract),
                reads=[rYA] + msbr, writes=[rYA])
        aswitch(arCDB)
        for j in range(DT):
            s = j % 8
            rw = R(f"ws{s}")
            p.dma_r("gp", "dwa", 4, lambda e, j=j, s=s: e.dma_start(
                wslot(s), op_blk[j]),
                reads=[R("cv_op")], writes=[rw], war=aclaim(arCDB, rw))
            pairs = [(wslot(s)[:, kc, :], HB[:, kc, :]) for kc in range(DT)]
            mmgrp(bank(j % 2), pairs, [PSB[j % 2]], [rw, rHB])
            p.emit("dv", lambda e, j=j: e.tensor_tensor(
                YA[:, j, :], YA[:, j, :], bank(j % 2), OP.add),
                reads=[PSB[j % 2]], writes=[rYA])
        ln_stats(YA, rYA, T2, "rms", 1e-6, 1024)
        r2bc, r2br = bcast(row(1024, T2), R("row1024"), 6, T2)
        for i in range(DT):
            p.emit("dv", lambda e, i=i: e.scalar_tensor_tensor(
                HB[:, i, :], YA[:, i, :], wffn[:, i:i + 1], r2bc,
                OP.mult, OP.mult),
                reads=[rYA, R("wffn")] + r2br, writes=[rHB])
        mlp(rHB, AF.Silu, NFF_A, attn_gate_loader, attn_down_loader, False)
        aswitch(arXA)
        p.dma_r("sp", "dld", 4, lambda e, t0=t0: e.dma_start(
            XA[:, :, :], x0v[:, :, t0:t0 + T2]),
            reads=[R("x0s")], writes=[rXA], war=aclaim(arXA, rXA))
        ln_stats(YA, rYA, T2, "ln", 1e-5, 0)
        l3c, l3cr = bcast(row(0, T2), R("row0"), 6, T2)
        l3m, l3mr = bcast(row(T2, T2), R("row0m"), 7, T2)
        for i in range(DT):
            p.emit("dv", lambda e, i=i: e.tensor_tensor(
                B1[:], YA[:, i, :], l3c, OP.mult),
                reads=[rYA] + l3cr, writes=[rB[0]])
            p.emit("dv", lambda e: e.tensor_tensor(B1[:], B1[:], l3m,
                                                   OP.subtract),
                   reads=[rB[0]] + l3mr, writes=[rB[0]])
            p.emit("dv", lambda e, i=i: e.tensor_tensor(
                XA[:, i, :], XA[:, i, :], B1[:], OP.add),
                reads=[rB[0]], writes=[rXA])
        ln_stats(XA, rXA, T2, "ln", 1e-5, 1024)
        l4c, l4cr = bcast(row(1024, T2), R("row1024"), 6, T2)
        l4m, l4mr = bcast(row(1024 + T2, T2), R("row1024m"), 7, T2)
        for i in range(DT):
            p.emit("dv", lambda e, i=i: e.tensor_tensor(
                B1[:], XA[:, i, :], l4c, OP.mult),
                reads=[rXA] + l4cr, writes=[rB[0]])
            p.emit("dv", lambda e, i=i: e.tensor_tensor(
                HB[:, i, :], B1[:], l4m, OP.subtract),
                reads=[rB[0]] + l4mr, writes=[rHB])
        p.dma_r("sp", "dsc", 4, lambda e, t0=t0: e.dma_start(
            x1v[:, :, t0:t0 + T2], XA[:, :, :]),
            reads=[rXA], writes=[R("x1s")])
        p.dma_r("sp", "dsc", 4, lambda e, t0=t0: e.dma_start(
            h2v[:, :, t0:t0 + T2], HB[:, :, :]),
            reads=[rHB], writes=[R("h2s")])

    run_dq(10 ** 9)
    for ch2 in range(NC2):
        t0 = ch2 * T2
        p.dma_r("sp", "dld", 4, lambda e, t0=t0: e.dma_start(
            HB[:, :, :], h2v[:, :, t0:t0 + T2]),
            reads=[R("h2s")], writes=[rHB])
        mlp(rHB, AF.Gelu_apprx_tanh, NFF_Q, q_gate_loader, q_down_loader, True)
        aswitch(arXA)
        p.dma_r("sp", "dld", 4, lambda e, t0=t0: e.dma_start(
            XA[:, :, :], x1v[:, :, t0:t0 + T2]),
            reads=[R("x1s")], writes=[rXA], war=aclaim(arXA, rXA))
        ln_stats(YA, rYA, T2, "ln", 1e-5, 0)
        l5c, l5cr = bcast(row(0, T2), R("row0"), 6, T2)
        l5m, l5mr = bcast(row(T2, T2), R("row0m"), 7, T2)
        for i in range(DT):
            p.emit("dv", lambda e, i=i: e.tensor_tensor(
                B1[:], YA[:, i, :], l5c, OP.mult),
                reads=[rYA] + l5cr, writes=[rB[0]])
            p.emit("dv", lambda e: e.tensor_tensor(B1[:], B1[:], l5m,
                                                   OP.subtract),
                   reads=[rB[0]] + l5mr, writes=[rB[0]])
            p.emit("dv", lambda e, i=i: e.tensor_tensor(
                XA[:, i, :], XA[:, i, :], B1[:], OP.add),
                reads=[rB[0]], writes=[rXA])
        for tb in range(T2 // 128):
            slot = tb % 2
            for bq in range(4):
                def tfn(e, tb=tb, bq=bq):
                    mm = None
                    for k in range(4):
                        dt_i = bq * 4 + k
                        mm = e.transpose(
                            psum[:, bq * 512 + 128 * k:bq * 512 + 128 * (k + 1)],
                            XA[:, dt_i, tb * 128:(tb + 1) * 128], ident[:])
                    return mm
                p.emit("pe", tfn, reads=[rXA, R("ident")], writes=[PSB[bq]])
                p.emit("act", lambda e, bq=bq, slot=slot: e.activation(
                    XT[:, slot, bq * 512:(bq + 1) * 512],
                    psum[:, bq * 512:(bq + 1) * 512], AF.Copy),
                    reads=[PSB[bq]], writes=[rXT[slot]])
            sl = slice(t0 + tb * 128, t0 + (tb + 1) * 128)
            p.dma_r("sp", "dou", 2, lambda e, sl=sl, slot=slot: e.dma_start(
                y_d[sl, :], XT[:, slot, :]),
                reads=[rXT[slot]], writes=[R("yout")])
    p.wait_all("sp")

    p.replay(nc)
    stack.close()
    return nc


_NC_CACHE = {}


def kernel(L0_attn_x, L0_attn_pre_norm_w, L0_attn_w_a, L0_attn_w_i,
           L0_attn_w_v, L0_attn_out_proj_w, L0_attn_decay_bias,
           L0_attn_ffn_norm_w, L0_attn_w_gate_up, L0_attn_w_down,
           L0_ffn_gate_W_packed, L0_ffn_gate_scales, L0_ffn_up_W_packed,
           L0_ffn_up_scales, L0_ffn_down_W_packed, L0_ffn_down_scales):
    x = np.asarray(L0_attn_x, dtype=np.float32)
    B, S, d = x.shape
    assert d == D
    n_cores = 8
    per_b = n_cores // B
    S4 = S // per_b

    if S4 not in _NC_CACHE:
        _NC_CACHE[S4] = build_program(S4)
    nc = _NC_CACHE[S4]

    def as32(a):
        return np.ascontiguousarray(np.asarray(a), dtype=np.float32)

    def col_tiles(v):
        return np.ascontiguousarray(as32(v).reshape(DT, 128).T)

    common = {
        "w_a": as32(L0_attn_w_a), "w_i": as32(L0_attn_w_i),
        "w_v": as32(L0_attn_w_v), "out_proj_w": as32(L0_attn_out_proj_w),
        "w_gate_up": as32(L0_attn_w_gate_up), "w_down": as32(L0_attn_w_down),
        "gate_W": np.ascontiguousarray(np.asarray(L0_ffn_gate_W_packed,
                                                  dtype=np.int32)),
        "gate_S": as32(L0_ffn_gate_scales),
        "up_W": np.ascontiguousarray(np.asarray(L0_ffn_up_W_packed,
                                                dtype=np.int32)),
        "up_S": as32(L0_ffn_up_scales),
        "down_W": np.ascontiguousarray(np.asarray(L0_ffn_down_W_packed,
                                                  dtype=np.int32)),
        "down_S": as32(L0_ffn_down_scales),
        "dbias": col_tiles(L0_attn_decay_bias),
        "wpre": col_tiles(L0_attn_pre_norm_w),
        "wffn": col_tiles(L0_attn_ffn_norm_w),
        "ident": np.eye(128, dtype=np.float32),
    }
    in_maps = []
    for c in range(n_cores):
        b = c // per_b
        pos = c % per_b
        m = dict(common)
        m["x"] = np.ascontiguousarray(x[b, pos * S4:(pos + 1) * S4, :])
        mk = np.zeros((128, 4), dtype=np.float32)
        mk[:, :pos] = 1.0
        m["mask"] = mk
        in_maps.append(m)

    res = run_bass_kernel_spmd(nc, in_maps, core_ids=list(range(n_cores)))
    out = np.empty((B, S, D), dtype=np.float32)
    for c in range(n_cores):
        b = c // per_b
        pos = c % per_b
        out[b, pos * S4:(pos + 1) * S4, :] = res.results[c]["y"]
    return out


if __name__ == "__main__":
    nc = build_program(4096)
    print("built OK")


# version 07
# speedup vs baseline: 1.0170x; 1.0170x over previous
import contextlib
import numpy as np
import ml_dtypes
import concourse.bass as bass
import concourse.mybir as mybir
from concourse.bass_utils import run_bass_kernel_spmd

F32 = mybir.dt.float32
BF16 = mybir.dt.bfloat16
I32 = mybir.dt.int32
AF = mybir.ActivationFunctionType
OP = mybir.AluOpType
AX = mybir.AxisListType
BFNP = ml_dtypes.bfloat16

D = 2048
DT = D // 128
NFF_A = 8192
NFF_Q = 16384
CH = 64
T1 = 512
T2 = 512
QW_W = 512


class Resrc:
    __slots__ = ("name", "w", "rs")

    def __init__(self, name):
        self.name = name
        self.w = None
        self.rs = []


class Prog:
    COMPUTE = ("pe", "dv", "act", "gp")

    def __init__(self):
        self.plans = {"pe": [], "dv": [], "act": [], "gp": [], "sp": []}
        self.cnt = {}
        self.vc = {k: {} for k in self.plans}
        self.semkeys = []
        self.rotc = {}

    def sem(self, key):
        if key not in self.cnt:
            self.cnt[key] = 0
            self.semkeys.append(key)
        return key

    def _wait(self, ek, semkey, val):
        if val <= 0 or self.vc[ek].get(semkey, 0) >= val:
            return
        self.vc[ek][semkey] = val
        self.plans[ek].append(("wait", semkey, val))

    def _edges(self, ek, reads, writes, war=()):
        need = {}
        for r in reads:
            if r.w is not None:
                k, v = r.w
                need[k] = max(need.get(k, 0), v)
        for w in list(writes) + list(war):
            if w.w is not None:
                k, v = w.w
                need[k] = max(need.get(k, 0), v)
            for (k, v) in w.rs:
                need[k] = max(need.get(k, 0), v)
        for k, v in need.items():
            self._wait(ek, k, v)

    def emit(self, ek, fn, reads=(), writes=(), semkey=None, inc=1, war=()):
        assert ek in self.COMPUTE
        semkey = self.sem(semkey or ek)
        self._edges(ek, reads, writes, war)
        self.cnt[semkey] += inc
        ev = (semkey, self.cnt[semkey])
        self.plans[ek].append(("op", fn, semkey, inc))
        for r in reads:
            r.rs.append(ev)
        for w in writes:
            w.w = ev
            w.rs = []
        return ev

    def dma_r(self, ek, semkey, rot, fn, reads=(), writes=(), war=()):
        return self.dma(ek, semkey, fn, reads=reads, writes=writes, rot=rot,
                        war=war)

    def dma(self, ek, semkey, fn, reads=(), writes=(), rot=1, war=()):
        if rot > 1:
            i = self.rotc.get(semkey, 0)
            self.rotc[semkey] = i + 1
            semkey = f"{semkey}{i % rot}"
        semkey = self.sem(semkey)
        self._edges(ek, reads, writes, war)
        self._wait(ek, semkey, self.cnt[semkey])
        self.cnt[semkey] += 16
        ev = (semkey, self.cnt[semkey])
        self.plans[ek].append(("op", fn, semkey, 16))
        for r in reads:
            r.rs.append(ev)
        for w in writes:
            w.w = ev
            w.rs = []
        return ev

    def wait_all(self, ek):
        for k in list(self.semkeys):
            self._wait(ek, k, self.cnt[k])

    def replay(self, nc):
        with contextlib.ExitStack() as stack:
            sems = {k: stack.enter_context(nc.semaphore(f"S_{k}"))
                    for k in self.semkeys}
            block = stack.enter_context(nc.Block())
            for ek, attr in (("sp", "sync"), ("pe", "tensor"),
                             ("dv", "vector"), ("act", "scalar"),
                             ("gp", "gpsimd")):
                plan = self.plans[ek]

                def body(eng, plan=plan):
                    for item in plan:
                        if item[0] == "wait":
                            eng.wait_ge(sems[item[1]], item[2])
                        else:
                            _, fn, semkey, inc = item
                            instr = fn(eng)
                            assert instr is not None
                            instr.then_inc(sems[semkey], inc)

                getattr(block, attr)(body)


def build_program(S4):
    assert S4 % T1 == 0 and S4 % T2 == 0 and T1 % CH == 0 and T2 % CH == 0
    NC1 = S4 // T1
    NC2 = S4 // T2
    CHN1 = T1 // CH
    CHN2 = T2 // CH
    nc = bass.Bass("TRN2", target_bir_lowering=False, debug=False,
                   disable_frame_to_traceback=True)
    p = Prog()
    res = {}

    def R(name):
        if name not in res:
            res[name] = Resrc(name)
        return res[name]

    def din(name, shape, dt=F32):
        return nc.dram_tensor(name, shape, dt, kind="ExternalInput").ap()

    x_d = din("x", [S4, D])
    wa_d = din("w_a", [D, D])
    wi_d = din("w_i", [D, D])
    wv_d = din("w_v", [D, D])
    op_d = din("out_proj_w", [D, D])
    wgu_d = din("w_gate_up", [D, 2 * NFF_A])
    wd_d = din("w_down", [NFF_A, D])
    gW_d = din("gate_W", [NFF_Q, D // 2], I32)
    gS_d = din("gate_S", [NFF_Q, D // 32])
    uW_d = din("up_W", [NFF_Q, D // 2], I32)
    uS_d = din("up_S", [NFF_Q, D // 32])
    dW_d = din("down_W", [D, NFF_Q // 2], I32)
    dS_d = din("down_S", [D, NFF_Q // 32])
    dbias_d = din("dbias", [128, DT])
    wpre_d = din("wpre", [128, DT])
    wffn_d = din("wffn", [128, DT])
    ident_d = din("ident", [128, 128])
    mask_d = din("mask", [128, 4])
    y_d = nc.dram_tensor("y", [S4, D], F32, kind="ExternalOutput").ap()

    def scr(name, shape, dt):
        return nc.dram_tensor(name, shape, dt, kind="Internal").ap()

    wa_blk = scr("wa_blk", [DT, 128, DT, 128], BF16)
    wi_blk = scr("wi_blk", [DT, 128, DT, 128], BF16)
    wv_blk = scr("wv_blk", [DT, 128, DT, 128], BF16)
    op_blk = scr("op_blk", [DT, 128, DT, 128], BF16)
    wgu_blk = scr("wgu_blk", [32, 128, DT, 512], BF16)
    wd_bf = scr("wd_bf", [NFF_A, D], BF16)
    dqgT = scr("dqgT", [32, 128, DT, 512], BF16)
    dquT = scr("dquT", [32, 128, DT, 512], BF16)
    dqdT = scr("dqdT", [128, 128, D], BF16)
    cd_scr = scr("cd_scr", [DT, 128, S4], F32)
    in_scr = scr("in_scr", [DT, 128, S4], F32)
    x0_scr = scr("x0_scr", [DT, 128, S4], F32)
    x1_scr = scr("x1_scr", [DT, 128, S4], F32)
    h2_scr = scr("h2_scr", [DT, 128, S4], BF16)
    c1r_scr = scr("c1r_scr", [1, S4], F32)
    m1r_scr = scr("m1r_scr", [1, S4], F32)
    cc_in = scr("cc_in", [128, 32], F32)
    cc_out = scr("cc_out", [512, 32], F32)

    cdv = cd_scr.rearrange("a p s -> p a s")
    inv = in_scr.rearrange("a p s -> p a s")
    x0v = x0_scr.rearrange("a p s -> p a s")
    x1v = x1_scr.rearrange("a p s -> p a s")
    h2v = h2_scr.rearrange("a p s -> p a s")

    stack = contextlib.ExitStack()

    def sb(name, shape, dt):
        return stack.enter_context(nc.sbuf_tensor(name, shape, dt))

    ident = sb("identS", [128, 128], F32)
    ones1 = sb("ones1", [1, 128], F32)
    onesb = sb("onesb", [128, 1], BF16)
    dbias = sb("dbiasS", [128, DT], F32)
    wpre = sb("wpreS", [128, DT], F32)
    wffn = sb("wffnS", [128, DT], F32)
    maskT = sb("maskT", [128, 4], F32)
    carry = sb("carry", [128, DT], F32)
    ctmp = sb("ctmp", [128, DT], F32)
    c0cur = sb("c0cur", [128, DT], F32)
    sumla = sb("sumla", [128, DT], F32)
    cfQ = sb("cfQ", [128, 32], F32)
    ccg = sb("ccg", [128, 4, 32], F32)
    cs8 = sb("cs8", [128, CHN2], F32)
    rows = sb("rows", [1, 3072], F32)
    XT = sb("XT", [128, 2, D], F32)
    XA = sb("XA", [128, DT, T2], F32)
    YA = sb("YA", [128, DT, T2], F32)
    CDB = sb("CDB", [128, DT, T2], F32)
    INB = sb("INB", [128, DT, T2], F32)
    HB = sb("HB", [128, DT, T2], BF16)
    IG = sb("IG", [128, 8, T2], BF16)
    IGf = IG[:].rearrange("p a b -> p (a b)").bitcast(F32)
    B2 = IGf[:, 0:T1]
    B3 = IGf[:, T1:2 * T1]
    B4s = [IGf[:, 2 * T1:3 * T1], IGf[:, 3 * T1:4 * T1]]
    B1t = sb("B1", [128, 2, T1], F32)
    B1 = B1t[:, 0, :]
    IGf = None
    B5 = sb("B5", [128, 16], F32)
    DMP = sb("DMP", [128, T2], BF16)
    identb = sb("identB", [128, 128], BF16)
    QWt = sb("QWt", [128, 2, QW_W], I32)
    QSC = sb("QSC", [128, 2, QW_W // 16], F32)
    QAB = sb("QAB", [128, QW_W], I32)
    QC = sb("QC", [128, 2, QW_W], I32)
    QD = sb("QD", [128, QW_W], F32)
    QF = sb("QF", [128, QW_W], F32)
    QO = sb("QOt", [128, 2, 2 * QW_W], BF16)
    QT = sb("QTt", [128, 1, 2 * QW_W], BF16)

    psum = stack.enter_context(nc.psum_tensor("ps", [128, 4096], F32))
    PSB = [R(f"psb{i}") for i in range(8)]

    def bank(i):
        return psum[:, 512 * i:512 * (i + 1)]

    CDW = CDB[:].rearrange("p a b -> p (a b)").bitcast(BF16).rearrange("p (a b) -> p a b", a=DT)
    INW = INB[:].rearrange("p a b -> p (a b)").bitcast(BF16).rearrange("p (a b) -> p a b", a=DT)
    INW8 = INB[:].rearrange("p a b -> p (a b)").bitcast(BF16).rearrange("p (a b) -> p a b", a=8)

    rXA, rYA, rCDB, rINB, rHB, rIG = (R("XA"), R("YA"), R("CDB"), R("INB"),
                                      R("HB"), R("IG"))

    class Ar:
        def __init__(self):
            self.cur = set()
            self.prev = []

    arCDB, arINB, arIG, arXA = Ar(), Ar(), Ar(), Ar()

    def aclaim(ar, r):
        if r in ar.cur:
            return []
        ar.cur.add(r)
        return ar.prev

    def aswitch(ar):
        if ar.cur:
            ar.prev = list(ar.cur)
            ar.cur = set()
    rXT = [R("XT0"), R("XT1")]
    rB1 = [R("B1_0"), R("B1_1")]
    rB4s = [R("B4_0"), R("B4_1")]
    rB = [rB1[0], R("B2"), R("B3"), rB4s[0], R("B5")]
    rDMP = R("DMP")
    XAW = (XA[:].rearrange("p a b -> p (a b)").bitcast(BF16)
           .rearrange("p (a b) -> p a b", a=DT))

    for t, d_src, rr in ((ident, ident_d, R("ident")),
                         (dbias, dbias_d, R("dbias")),
                         (wpre, wpre_d, R("wpre")), (wffn, wffn_d, R("wffn")),
                         (maskT, mask_d, R("maskT"))):
        p.dma_r("sp", "dxa", 2, lambda e, t=t, s=d_src: e.dma_start(t[:], s),
              writes=[rr])
    p.emit("act", lambda e: e.activation(identb[:], ident[:], AF.Copy),
           reads=[R("ident")], writes=[R("identb")])
    p.emit("dv", lambda e: e.memset(ones1[:], 1.0), writes=[R("ones1")])
    p.emit("dv", lambda e: e.memset(onesb[:], 1.0), writes=[R("onesb")])
    p.emit("dv", lambda e: e.memset(c0cur[:], 0.0), writes=[R("c0cur")])
    p.emit("dv", lambda e: e.memset(sumla[:], 0.0), writes=[R("sumla")])
    for cval in (1e-5, 1e-6):
        cten = sb(f"cst{cval:.0e}", [128, 1], F32)
        nc.const_aps.register(F32, cval, cten.ap())
        p.emit("dv", lambda e, cten=cten, cval=cval: e.memset(cten[:], cval),
               writes=[R("consts")])
    _initev = ("dv", p.cnt.get("dv", 0))
    for _ek in ("pe", "act", "gp", "sp"):
        p._wait(_ek, "dv", _initev[1])

    for i in range(4):
        sl = slice(i * (NFF_A // 4), (i + 1) * (NFF_A // 4))
        p.dma_r("gp", "gcv", 2,
                lambda e, sl=sl: e.dma_start(wd_bf[sl, :], wd_d[sl, :]),
                writes=[R("cv_wd")])
    CONV = CDW
    for nm, (w_src, blkt) in {"wa": (wa_d, wa_blk), "wi": (wi_d, wi_blk),
                              "wv": (wv_d, wv_blk), "op": (op_d, op_blk)}.items():
        src3f = w_src.rearrange("(a p) n -> p a n", p=128)
        for j in range(DT):
            s = j % 8
            rcv = R(f"cvb{s}")
            p.dma_r("gp", "gcv", 2,
                    lambda e, j=j, s=s, src3f=src3f: e.dma_start(
                        CONV[:, :, 128 * s:128 * (s + 1)],
                        src3f[:, :, j * 128:(j + 1) * 128]),
                    writes=[rcv], war=aclaim(arCDB, rcv))
            p.dma_r("sp", "gcs", 2,
                    lambda e, j=j, s=s, blkt=blkt: e.dma_start(
                        blkt[j].rearrange("p a n -> p (a n)")
                        .rearrange("p (a n) -> p a n", a=DT),
                        CONV[:, :, 128 * s:128 * (s + 1)]),
                    reads=[rcv], writes=[R(f"cv_{nm}")])
    aswitch(arCDB)
    wgu3f = wgu_d.rearrange("(a p) n -> p a n", p=128)
    for lg in range(32):
        col0 = lg * 512
        half = lg % 2
        rcv = R(f"cvg{half}")
        p.dma_r("gp", "gcv", 2,
                lambda e, col0=col0, half=half: e.dma_start(
                    CDW[:, :, 512 * half:512 * (half + 1)],
                    wgu3f[:, :, col0:col0 + 512]),
                writes=[rcv], war=aclaim(arCDB, rcv))
        p.dma_r("sp", "gcs", 2,
                lambda e, lg=lg, half=half: e.dma_start(
                    wgu_blk[lg], CDW[:, :, 512 * half:512 * (half + 1)]),
                reads=[rcv], writes=[R("cv_wgu")])

    dq_queue = []
    rdq = {"g": R("dq_g"), "u": R("dq_u"), "d": R("dq_d")}
    psumb = psum[:].rearrange("p (a b) -> p a b", a=8).bitcast(BF16)

    def dequant_tile(W_src, S_src, dst_fn, r0, c0, slot, rtag):
        qw = QWt[:, slot, :]
        qsc = QSC[:, slot, :]
        qo = QO[:, slot, :]
        qt = QT[:, 0, :]
        dqb = 6 + slot
        rqw, rqs = R(f"qw{slot}"), R(f"qs{slot}")
        rqo, rqt = R(f"qo{slot}"), R("qt0")
        a = b = QAB[:]
        d, f = QD[:], QF[:]
        ra = rb = R("q_ab")
        rd, rf = R("q_d"), R("q_f")

        def plan():
            p.dma_r("sp", "gdq", 2,
                  lambda e: e.dma_start(qw, W_src[r0:r0 + 128, c0:c0 + QW_W]),
                  writes=[rqw])
            p.dma_r("sp", "gdq", 2,
                  lambda e: e.dma_start(
                      qsc, S_src[r0:r0 + 128, c0 // 16:(c0 + QW_W) // 16]),
                  writes=[rqs])
            p.emit("dv", lambda e: e.tensor_copy(
                f.rearrange("p (k r) -> p k r", r=16),
                qsc.unsqueeze(2).broadcast_to([128, QW_W // 16, 16])),
                reads=[rqs], writes=[rf])
            for half in range(2):
                c = QC[:, slot, :]
                rc = R(f"qc{slot}")
                if half == 0:
                    p.emit("dv", lambda e: e.tensor_scalar(
                        a, qw, 7, None, OP.bitwise_and),
                        reads=[rqw], writes=[ra])
                else:
                    p.emit("dv", lambda e: e.tensor_scalar(
                        a, qw, 4, 7, OP.logical_shift_right, OP.bitwise_and),
                        reads=[rqw], writes=[ra])
                p.emit("gp", lambda e, c=c: e.tensor_scalar(
                    c, a, float(1 << 22), float(0x3F000000), OP.mult, OP.add),
                    reads=[ra], writes=[rc])
                p.emit("dv", lambda e: e.tensor_scalar(
                    d, a, -0.25, 0.5, OP.mult, OP.add),
                    reads=[ra], writes=[rd])
                p.emit("act", lambda e: e.activation(d, d, AF.Relu),
                       reads=[rd], writes=[rd])
                eng_s = "dv" if half == 0 else "gp"
                p.emit(eng_s, lambda e, c=c: e.tensor_tensor(
                    c.bitcast(F32), c.bitcast(F32), d, OP.subtract),
                    reads=[rc, rd], writes=[rc])
                if half == 0:
                    p.emit("dv", lambda e: e.tensor_scalar(
                        b, qw, 8, 28, OP.bitwise_and, OP.logical_shift_left),
                        reads=[rqw], writes=[rb])
                else:
                    p.emit("dv", lambda e: e.tensor_scalar(
                        b, qw, 128, 24, OP.bitwise_and, OP.logical_shift_left),
                        reads=[rqw], writes=[rb])
                p.emit("dv", lambda e, c=c: e.tensor_tensor(
                    c, c, b, OP.bitwise_or),
                    reads=[rc, rb], writes=[rc])
                qo_h = qo.rearrange("p (k two) -> p k two", two=2)[:, :, half]
                p.emit("dv", lambda e, c=c, qo_h=qo_h: e.tensor_tensor(
                    qo_h, c.bitcast(F32), f, OP.mult),
                    reads=[rc, rf], writes=[rqo])

            def tfn(e):
                mm = None
                for t in range(8):
                    mm = e.transpose(
                        psumb[:, dqb, 128 * t:128 * (t + 1)],
                        qo[:, 128 * t:128 * (t + 1)], identb[:])
                return mm
            p.emit("pe", tfn, reads=[rqo, R("identb")], writes=[PSB[dqb]])
            p.emit("act", lambda e: e.activation(qt, psumb[:, dqb, :],
                                                 AF.Copy),
                   reads=[PSB[dqb]], writes=[rqt])
            p.dma_r("act", "gds", 2,
                  lambda e: e.dma_start(dst_fn(), qt.rearrange(
                      "p (a n) -> p a n", a=8)),
                  reads=[rqt], writes=[rtag])
        dq_queue.append(plan)

    dqdTv = dqdT.rearrange("j p d -> p j d")
    sl_i = [0]
    for (Wsrc, Ssrc, dqT, nrows, words, key) in (
            (gW_d, gS_d, dqgT, NFF_Q, D // 2, "g"),
            (uW_d, uS_d, dquT, NFF_Q, D // 2, "u"),
            (dW_d, dS_d, None, D, NFF_Q // 2, "d")):
        for r0 in range(0, nrows, 128):
            for c0 in range(0, words, QW_W):
                if key == "d":
                    j0 = (2 * c0) // 128
                    dst_fn = (lambda j0=j0, r0=r0:
                              dqdTv[:, j0:j0 + 8, r0:r0 + 128])
                else:
                    lg, n0 = r0 // 512, r0 % 512
                    hc0 = (2 * c0) // 128
                    dst_fn = (lambda dqT=dqT, lg=lg, hc0=hc0, n0=n0:
                              dqT[lg, :, hc0:hc0 + 8, n0:n0 + 128])
                dequant_tile(Wsrc, Ssrc, dst_fn, r0, c0, sl_i[0] % 2,
                             rdq[key])
                sl_i[0] += 1
    DQ_TOTAL = len(dq_queue)
    dq_done = [0]

    def run_dq(n):
        while n > 0 and dq_done[0] < DQ_TOTAL:
            dq_queue[dq_done[0]]()
            dq_done[0] += 1
            n -= 1

    def mmgrp(out_ap, pairs, bank_res_list, extra_reads, first=True,
              last=True):
        def fn(e):
            mm = None
            n = len(pairs)
            for i, (lt, rt) in enumerate(pairs):
                mm = e.matmul(out_ap, lt, rt, start=(first and i == 0),
                              stop=(last and i == n - 1))
            return mm
        return p.emit("pe", fn, reads=list(extra_reads), writes=bank_res_list)

    def row(off, t):
        return rows[0:1, off:off + t]

    def ln_stats(arena, arena_res, t_w, kind, eps, off):
        want_mean = kind != "rms"
        rsumv = psum[0:1, 6 * 512:6 * 512 + t_w]
        rsqv = psum[0:1, 7 * 512:7 * 512 + t_w]
        for i in range(DT):
            p.emit("act", lambda e, i=i: e.activation(
                DMP[:, 0:t_w], arena[:, i, 0:t_w], AF.Square),
                reads=[arena_res], writes=[rDMP])
            mmgrp(rsqv, [(onesb[:, :], DMP[:, 0:t_w])], [PSB[7]],
                  [rDMP, R("onesb")], first=(i == 0), last=(i == DT - 1))
            if want_mean:
                p.emit("act", lambda e, i=i: e.activation(
                    DMP[:, 0:t_w], arena[:, i, 0:t_w], AF.Copy),
                    reads=[arena_res], writes=[rDMP])
                mmgrp(rsumv, [(onesb[:, :], DMP[:, 0:t_w])], [PSB[6]],
                      [rDMP, R("onesb")], first=(i == 0), last=(i == DT - 1))
        if kind == "rms":
            ro = row(off, t_w)
            rr = R(f"row{off}")
            p.emit("act", lambda e: e.activation(
                ro, rsqv, AF.Ln, bias=float(eps), scale=1.0 / D),
                reads=[PSB[7]], writes=[rr])
            p.emit("act", lambda e: e.activation(ro, ro, AF.Exp, scale=-0.5),
                   reads=[rr], writes=[rr])
            return
        mu = row(2048, t_w)
        var = row(2560, t_w)
        rmu, rvar = R("rowmu"), R("rowvar")
        p.emit("act", lambda e: e.activation(mu, rsumv, AF.Copy, scale=1.0 / D),
               reads=[PSB[6]], writes=[rmu])
        p.emit("act", lambda e: e.activation(var, mu, AF.Square),
               reads=[rmu], writes=[rvar])
        p.emit("dv", lambda e: e.scalar_tensor_tensor(
            var, rsqv, 1.0 / D, var, OP.mult, OP.subtract),
            reads=[PSB[7], rvar], writes=[rvar])
        co = row(off, t_w)
        mo = row(off + t_w, t_w)
        rco, rmo = R(f"row{off}"), R(f"row{off}m")
        if kind == "ln":
            p.emit("act", lambda e: e.activation(co, var, AF.Ln,
                                                 bias=float(eps)),
                   reads=[rvar], writes=[rco])
            p.emit("act", lambda e: e.activation(co, co, AF.Exp, scale=-0.5),
                   reads=[rco], writes=[rco])
        else:
            s = row(1024, t_w)
            t = row(2560, t_w)
            ms = row(1536, t_w)
            rs_, rt_, rms_ = R("rowS"), R("rowT"), R("rowMS")
            p.emit("act", lambda e: e.activation(s, var, AF.Ln, bias=1e-5),
                   reads=[rvar], writes=[rs_])
            p.emit("act", lambda e: e.activation(s, s, AF.Exp, scale=-0.5),
                   reads=[rs_], writes=[rs_])
            p.emit("dv", lambda e: e.tensor_tensor(ms, mu, s, OP.mult),
                   reads=[rmu, rs_], writes=[rms_])
            p.emit("dv", lambda e: e.tensor_tensor(t, var, s, OP.mult),
                   reads=[rvar, rs_], writes=[rt_])
            p.emit("dv", lambda e: e.tensor_tensor(t, t, s, OP.mult),
                   reads=[rt_, rs_], writes=[rt_])
            p.emit("act", lambda e: e.activation(t, t, AF.Ln, bias=1e-6),
                   reads=[rt_], writes=[rt_])
            p.emit("act", lambda e: e.activation(t, t, AF.Exp, scale=-0.5),
                   reads=[rt_], writes=[rt_])
            p.emit("dv", lambda e: e.tensor_tensor(co, s, t, OP.mult),
                   reads=[rs_, rt_], writes=[rco])
        p.emit("dv", lambda e: e.tensor_tensor(mo, mu, co, OP.mult),
               reads=[rmu, rco], writes=[rmo])

    def bcast(row_ap, row_res, bank_i, t_w):
        nb = (t_w + 511) // 512
        outs = [PSB[bank_i + i] for i in range(nb)]

        def fn(e):
            mm = None
            for i in range(nb):
                w = min(512, t_w - i * 512)
                mm = e.matmul(
                    psum[:, 512 * (bank_i + i):512 * (bank_i + i) + w],
                    ones1[:, :], row_ap[:, i * 512:i * 512 + w],
                    start=True, stop=True)
            return mm
        p.emit("pe", fn, reads=[row_res, R("ones1")], writes=outs)
        return psum[:, 512 * bank_i:512 * bank_i + t_w], outs


    def wslot(s):
        return CDW.rearrange("p a (s c) -> p a s c", c=128)[:, :, s, :]

    aswitch(arCDB)
    for ch1 in range(NC1):
        t0 = ch1 * T1
        for tb in range(T1 // 128):
            sl = slice(t0 + tb * 128, t0 + (tb + 1) * 128)
            slot = tb % 2
            p.dma_r("sp", "dxa", 2,
                  lambda e, sl=sl, slot=slot: e.dma_start(XT[:, slot, :],
                                                          x_d[sl, :]),
                  writes=[rXT[slot]])
            for bq in range(4):
                def tfn(e, slot=slot, bq=bq):
                    mm = None
                    for k in range(4):
                        dt_i = bq * 4 + k
                        mm = e.transpose(
                            psum[:, bq * 512 + 128 * k:bq * 512 + 128 * (k + 1)],
                            XT[:, slot, 128 * dt_i:128 * (dt_i + 1)], ident[:])
                    return mm
                p.emit("pe", tfn, reads=[rXT[slot], R("ident")],
                       writes=[PSB[bq]])
                p.emit("act", lambda e, bq=bq, tb=tb: e.activation(
                    XA[:, bq * 4:(bq + 1) * 4, tb * 128:(tb + 1) * 128],
                    psum[:, bq * 512:(bq + 1) * 512]
                    .rearrange("p (a b) -> p a b", a=4), AF.Copy),
                    reads=[PSB[bq]], writes=[rXA])
        p.dma_r("sp", "dsc", 4,
              lambda e, t0=t0: e.dma_start(x0v[:, :, t0:t0 + T1],
                                           XA[:, :, 0:T1]),
              reads=[rXA], writes=[R("x0s")])
        ln_stats(XA, rXA, T1, "ln1", None, 0)
        p.dma_r("sp", "dsc", 4, lambda e, t0=t0: e.dma_start(
            c1r_scr[0:1, t0:t0 + T1], row(1024, T1)),
            reads=[R("rowS")], writes=[R("c1rs")])
        p.dma_r("sp", "dsc", 4, lambda e, t0=t0: e.dma_start(
            m1r_scr[0:1, t0:t0 + T1], row(1536, T1)),
            reads=[R("rowMS")], writes=[R("m1rs")])
        cbc, cbr = bcast(row(0, T1), R("row0"), 6, T1)
        mbc, mbr = bcast(row(T1, T1), R("row0m"), 7, T1)
        for i in range(DT):
            p.emit("dv", lambda e, i=i: e.tensor_tensor(
                B1[:], XA[:, i, 0:T1], cbc, OP.mult),
                reads=[rXA] + cbr, writes=[rB[0]])
            p.emit("dv", lambda e: e.tensor_tensor(B1[:], B1[:], mbc,
                                                   OP.subtract),
                   reads=[rB[0]] + mbr, writes=[rB[0]])
            p.emit("dv", lambda e, i=i: e.tensor_scalar(
                HB[:, i, 0:T1], B1[:], wpre[:, i:i + 1], None, OP.mult),
                reads=[rB[0], R("wpre")], writes=[rHB])
        for j in range(DT):
            s0 = (j % 2) * 3
            rws = [R(f"ws{s0 + k}") for k in range(3)]
            for k, (wblk, rw, cv) in enumerate(
                    ((wa_blk, rws[0], "cv_wa"), (wi_blk, rws[1], "cv_wi"),
                     (wv_blk, rws[2], "cv_wv"))):
                p.dma_r("gp", "dwa", 4,
                      lambda e, wblk=wblk, j=j, s=s0 + k: e.dma_start(
                          wslot(s), wblk[j]),
                      reads=[R(cv)], writes=[rw], war=aclaim(arCDB, rw))
            ba, bi, bv = (j % 2), 2 + (j % 2), 4 + (j % 2)
            for (s, rw, bk) in ((s0, rws[0], ba), (s0 + 1, rws[1], bi),
                                (s0 + 2, rws[2], bv)):
                pairs = [(wslot(s)[:, kc, :], HB[:, kc, 0:T1])
                         for kc in range(DT)]
                mmgrp(bank(bk), pairs, [PSB[bk]], [rw, rHB])
            Bj = B1t[:, j % 2, :]
            rBj = rB1[j % 2]
            B4, B2b = B4s[j % 2], B4s[1 - j % 2]
            rB4, rB2b = rB4s[j % 2], rB4s[1 - j % 2]
            p.emit("act", lambda e, ba=ba, j=j, Bj=Bj: e.activation(
                Bj, bank(ba), AF.Sigmoid, bias=dbias[:, j:j + 1]),
                reads=[PSB[ba], R("dbias")], writes=[rBj])
            p.emit("act", lambda e, bi=bi: e.activation(
                B2, bank(bi), AF.Sigmoid),
                reads=[PSB[bi]], writes=[rB[1]], war=aclaim(arIG, rB[1]))
            p.emit("gp", lambda e, Bj=Bj: e.tensor_tensor(
                B3, Bj, Bj, OP.mult),
                reads=[rBj], writes=[rB[2]], war=aclaim(arIG, rB[2]))
            p.emit("act", lambda e: e.activation(B3, B3, AF.Ln,
                                                 bias=1.0, scale=-1.0),
                   reads=[rB[2]], writes=[rB[2]])
            p.emit("act", lambda e: e.activation(B3, B3, AF.Exp, scale=0.5),
                   reads=[rB[2]], writes=[rB[2]])
            p.emit("dv", lambda e, bv=bv, B4=B4: e.tensor_tensor(
                B4, B2, bank(bv), OP.mult),
                reads=[rB[1], PSB[bv]], writes=[rB4],
                war=aclaim(arIG, rB4))
            p.emit("gp", lambda e, B4=B4: e.tensor_tensor(B4, B3, B4,
                                                          OP.mult),
                   reads=[rB[2], rB4], writes=[rB4])
            p.emit("act", lambda e, Bj=Bj, B2b=B2b: e.activation(
                B2b, Bj, AF.Ln),
                reads=[rBj], writes=[rB2b], war=aclaim(arIG, rB2b))
            for cc in range(CHN1):
                sl = slice(cc * CH, (cc + 1) * CH)
                p.emit("dv", lambda e, sl=sl, B2b=B2b: e.tensor_tensor_scan(
                    B2b[:, sl], B2b[:, sl], B2b[:, sl], 0.0, OP.add,
                    OP.bypass),
                    reads=[rB2b], writes=[rB2b])
            p.emit("dv", lambda e, B2b=B2b: e.reduce_sum(
                B5[:, 0:1], B2b[:, CH - 1::CH], AX.X),
                reads=[rB2b], writes=[rB[4]])
            p.emit("gp", lambda e, j=j: e.tensor_tensor(
                sumla[:, j:j + 1], sumla[:, j:j + 1], B5[:, 0:1], OP.add),
                reads=[rB[4]], writes=[R("sumla")])
            p.emit("act", lambda e, Bj=Bj, B2b=B2b: e.activation(
                Bj, B2b, AF.Exp),
                reads=[rB2b], writes=[rBj])
            p.emit("dv", lambda e, B2b=B2b: e.tensor_scalar(
                B2b, B2b, -23.025850929940457, None, OP.max),
                reads=[rB2b], writes=[rB2b])
            p.emit("act", lambda e, B2b=B2b: e.activation(
                B2b, B2b, AF.Exp, scale=-1.0),
                reads=[rB2b], writes=[rB2b])
            p.emit("dv", lambda e, B4=B4, B2b=B2b: e.tensor_tensor(
                B4, B4, B2b, OP.mult),
                reads=[rB4, rB2b], writes=[rB4])
            for cc in range(CHN1):
                sl = slice(cc * CH, (cc + 1) * CH)
                p.emit("dv", lambda e, sl=sl, B4=B4: e.tensor_tensor_scan(
                    B4[:, sl], B4[:, sl], B4[:, sl], 0.0, OP.add, OP.bypass),
                    reads=[rB4], writes=[rB4])
            p.emit("dv", lambda e, Bj=Bj, B4=B4: e.tensor_tensor(
                B4, Bj, B4, OP.mult),
                reads=[rBj, rB4], writes=[rB4])
            p.emit("dv", lambda e, j=j, Bj=Bj, B4=B4: e.tensor_tensor_scan(
                B5[:, 1:1 + CHN1], Bj[:, CH - 1::CH], B4[:, CH - 1::CH],
                c0cur[:, j:j + 1], OP.mult, OP.add),
                reads=[rBj, rB4, R("c0cur")], writes=[rB[4]])
            p.emit("gp", lambda e, j=j: e.tensor_copy(
                c0cur[:, j:j + 1], B5[:, CHN1:CHN1 + 1]),
                reads=[rB[4]], writes=[R("c0cur")])
            p.dma_r("sp", "dsc", 4,
                  lambda e, j=j, t0=t0, Bj=Bj: e.dma_start(
                      cdv[:, j, t0:t0 + T1], Bj),
                  reads=[rBj], writes=[R("cds")])
            p.dma_r("sp", "dsc", 4,
                  lambda e, j=j, t0=t0, B4=B4: e.dma_start(
                      inv[:, j, t0:t0 + T1], B4),
                  reads=[rB4], writes=[R("ins")])
            run_dq(2)

    p.emit("dv", lambda e: e.tensor_copy(cfQ[:, 0:DT], c0cur[:]),
           reads=[R("c0cur")], writes=[R("cfQ")])
    p.emit("act", lambda e: e.activation(cfQ[:, DT:2 * DT], sumla[:], AF.Exp),
           reads=[R("sumla")], writes=[R("cfQ")])
    p.dma("sp", "dcc", lambda e: e.dma_start(cc_in[:, :], cfQ[:]),
          reads=[R("cfQ")], writes=[R("ccin")])
    p.emit("gp", lambda e: e.collective_compute(
        "AllGather", OP.bypass, replica_groups=[[0, 1, 2, 3], [4, 5, 6, 7]],
        ins=[cc_in.opt()], outs=[cc_out.opt()]),
        reads=[R("ccin")], writes=[R("ccout")], semkey="ccs", inc=1)
    ccov = cc_out.rearrange("(r p) c -> p r c", p=128)
    p.dma("sp", "dcc", lambda e: e.dma_start(ccg[:], ccov),
          reads=[R("ccout")], writes=[R("ccg")])
    p.emit("dv", lambda e: e.memset(carry[:], 0.0), writes=[R("carry")])
    for r in range(4):
        p.emit("dv", lambda e, r=r: e.tensor_tensor(
            ctmp[:], carry[:], ccg[:, r, DT:2 * DT], OP.mult),
            reads=[R("carry"), R("ccg")], writes=[R("ctmp")])
        p.emit("dv", lambda e, r=r: e.tensor_tensor(
            ctmp[:], ctmp[:], ccg[:, r, 0:DT], OP.add),
            reads=[R("ctmp"), R("ccg")], writes=[R("ctmp")])
        p.emit("dv", lambda e: e.tensor_tensor(
            ctmp[:], ctmp[:], carry[:], OP.subtract),
            reads=[R("ctmp"), R("carry")], writes=[R("ctmp")])
        p.emit("dv", lambda e, r=r: e.scalar_tensor_tensor(
            carry[:], ctmp[:], maskT[:, r:r + 1], carry[:], OP.mult, OP.add),
            reads=[R("ctmp"), R("maskT")], writes=[R("carry")])

    def mlp(rhs_res, act_f, nff, gate_loader, down_loader, first_copy):
        aswitch(arCDB)
        aswitch(arINB)
        aswitch(arXA)
        aswitch(arIG)
        n_sg = nff // 1024
        for sg in range(n_sg):
            for lg in range(2):
                par = (sg * 2 + lg) % 2
                W = CDW if par == 0 else XAW
                ar = arCDB if par == 0 else arXA
                jj0 = sg * 8 + lg * 4
                rg, ru = R(f"wgt_g{par}"), R(f"wgt_u{par}")
                gate_loader(jj0, W, ar, rg, ru)
                for k in range(4):
                    jj = lg * 4 + k
                    bg, bu = (jj % 2), 2 + (jj % 2)
                    gp_ = [(W[:, kc, 128 * k:128 * (k + 1)], HB[:, kc, :])
                           for kc in range(DT)]
                    up_ = [(W[:, kc, 512 + 128 * k:512 + 128 * (k + 1)],
                            HB[:, kc, :]) for kc in range(DT)]
                    mmgrp(bank(bg), gp_, [PSB[bg]], [rg, rhs_res])
                    mmgrp(bank(bu), up_, [PSB[bu]], [ru, rhs_res])
                    p.emit("act", lambda e, bg=bg: e.activation(
                        B1[:, 0:T2], bank(bg), act_f),
                        reads=[PSB[bg]], writes=[rB[0]])
                    p.emit("dv", lambda e, bu=bu, jj=jj: e.tensor_tensor(
                        IG[:, jj, :], B1[:, 0:T2], bank(bu), OP.mult),
                        reads=[rB[0], PSB[bu]], writes=[rIG],
                        war=aclaim(arIG, rIG))
            rwd = [R(f"wd{s}") for s in range(8)]
            for s in range(8):
                down_loader(s, sg * 8 + s, rwd[s])
            for q in range(4):
                def dfn(e, q=q):
                    mm = None
                    for jj in range(8):
                        for k in range(4):
                            dcc = q * 4 + k
                            mm = e.matmul(
                                bank(4 + k),
                                INW8[:, jj, 128 * dcc:128 * (dcc + 1)],
                                IG[:, jj, :],
                                start=(jj == 0), stop=(jj == 7))
                    return mm
                p.emit("pe", dfn, reads=rwd + [rIG],
                       writes=[PSB[4 + i] for i in range(4)])
                dst = YA[:, q * 4:(q + 1) * 4, :]
                srcv = psum[:, 4 * 512:8 * 512].rearrange("p (a b) -> p a b",
                                                          a=4)
                if first_copy and sg == 0:
                    p.emit("dv", lambda e, dst=dst, srcv=srcv: e.tensor_copy(
                        dst, srcv),
                        reads=[PSB[4 + i] for i in range(4)], writes=[rYA])
                else:
                    p.emit("dv", lambda e, dst=dst, srcv=srcv: e.tensor_tensor(
                        dst, dst, srcv, OP.add),
                        reads=[PSB[4 + i] for i in range(4)], writes=[rYA])
            run_dq(8)

    def attn_gate_loader(jj0, W, ar, rg, ru):
        p.dma_r("gp", "dwg", 4, lambda e, jj0=jj0, W=W: e.dma_start(
            W[:, :, 0:512], wgu_blk[jj0 // 4]),
            reads=[R("cv_wgu")], writes=[rg], war=aclaim(ar, rg))
        p.dma_r("gp", "dwg", 4, lambda e, jj0=jj0, W=W: e.dma_start(
            W[:, :, 512:1024], wgu_blk[16 + jj0 // 4]),
            reads=[R("cv_wgu")], writes=[ru], war=aclaim(ar, ru))

    def attn_down_loader(s, jj_abs, rw):
        p.dma_r("sp", "dwg", 4, lambda e, s=s, jj_abs=jj_abs: e.dma_start(
            INW8[:, s, :], wd_bf[jj_abs * 128:(jj_abs + 1) * 128, :]),
            reads=[R("cv_wd")], writes=[rw], war=aclaim(arINB, rw))

    def q_gate_loader(jj0, W, ar, rg, ru):
        p.dma_r("gp", "dwg", 4, lambda e, jj0=jj0, W=W: e.dma_start(
            W[:, :, 0:512], dqgT[jj0 // 4]),
            reads=[rdq["g"]], writes=[rg], war=aclaim(ar, rg))
        p.dma_r("gp", "dwg", 4, lambda e, jj0=jj0, W=W: e.dma_start(
            W[:, :, 512:1024], dquT[jj0 // 4]),
            reads=[rdq["u"]], writes=[ru], war=aclaim(ar, ru))

    def q_down_loader(s, jj_abs, rw):
        p.dma_r("sp", "dwg", 4, lambda e, s=s, jj_abs=jj_abs: e.dma_start(
            INW8[:, s, :], dqdT[jj_abs]),
            reads=[rdq["d"]], writes=[rw], war=aclaim(arINB, rw))

    for ch2 in range(NC2):
        t0 = ch2 * T2
        aswitch(arCDB)
        aswitch(arINB)
        p.dma_r("sp", "dld", 4, lambda e, t0=t0: e.dma_start(
            CDB[:, :, :], cdv[:, :, t0:t0 + T2]),
            reads=[R("cds")], writes=[rCDB], war=aclaim(arCDB, rCDB))
        p.dma_r("sp", "dld", 4, lambda e, t0=t0: e.dma_start(
            INB[:, :, :], inv[:, :, t0:t0 + T2]),
            reads=[R("ins")], writes=[rINB], war=aclaim(arINB, rINB))
        p.dma_r("sp", "dld", 4, lambda e, t0=t0: e.dma_start(
            XA[:, :, :], x0v[:, :, t0:t0 + T2]),
            reads=[R("x0s")], writes=[rXA], war=aclaim(arXA, rXA))
        for j in range(DT):
            p.emit("dv", lambda e, j=j: e.tensor_tensor_scan(
                cs8[:, :], CDB[:, j, CH - 1::CH], INB[:, j, CH - 1::CH],
                carry[:, j:j + 1], OP.mult, OP.add),
                reads=[rCDB, rINB, R("carry")], writes=[R("cs8")])
            p.emit("dv", lambda e, j=j: e.scalar_tensor_tensor(
                HB[:, j, 0:CH], CDB[:, j, 0:CH], carry[:, j:j + 1],
                INB[:, j, 0:CH], OP.mult, OP.add),
                reads=[rCDB, rINB, R("carry")], writes=[rHB])
            p.emit("dv", lambda e, j=j: e.tensor_tensor(
                B1[:].rearrange("p (a b) -> p a b", a=CHN2)[:, 1:, :],
                CDB[:, j, CH:].rearrange("p (a b) -> p a b", a=CHN2 - 1),
                cs8[:, 0:CHN2 - 1].unsqueeze(2)
                .broadcast_to([128, CHN2 - 1, CH]), OP.mult),
                reads=[rCDB, R("cs8")], writes=[rB[0]])
            p.emit("dv", lambda e, j=j: e.tensor_tensor(
                HB[:, j, CH:], B1[:, CH:], INB[:, j, CH:], OP.add),
                reads=[rB[0], rINB], writes=[rHB])
            p.emit("dv", lambda e, j=j: e.tensor_copy(
                carry[:, j:j + 1], cs8[:, CHN2 - 1:CHN2]),
                reads=[R("cs8")], writes=[R("carry")])
        p.dma_r("sp", "dld", 4, lambda e, t0=t0: e.dma_start(
            row(0, T2), c1r_scr[0:1, t0:t0 + T2]),
            reads=[R("c1rs")], writes=[R("row0")])
        p.dma_r("sp", "dld", 4, lambda e, t0=t0: e.dma_start(
            row(512, T2), m1r_scr[0:1, t0:t0 + T2]),
            reads=[R("m1rs")], writes=[R("row0m")])
        sbc, sbr = bcast(row(0, T2), R("row0"), 6, T2)
        msbc, msbr = bcast(row(512, T2), R("row0m"), 7, T2)
        for i in range(DT):
            p.emit("dv", lambda e, i=i: e.tensor_tensor(
                YA[:, i, :], XA[:, i, :], sbc, OP.mult),
                reads=[rXA] + sbr, writes=[rYA])
            p.emit("dv", lambda e, i=i: e.tensor_tensor(
                YA[:, i, :], YA[:, i, :], msbc, OP.subtract),
                reads=[rYA] + msbr, writes=[rYA])
        aswitch(arCDB)
        for j in range(DT):
            s = j % 8
            rw = R(f"ws{s}")
            p.dma_r("gp", "dwa", 4, lambda e, j=j, s=s: e.dma_start(
                wslot(s), op_blk[j]),
                reads=[R("cv_op")], writes=[rw], war=aclaim(arCDB, rw))
            pairs = [(wslot(s)[:, kc, :], HB[:, kc, :]) for kc in range(DT)]
            mmgrp(bank(j % 2), pairs, [PSB[j % 2]], [rw, rHB])
            p.emit("dv", lambda e, j=j: e.tensor_tensor(
                YA[:, j, :], YA[:, j, :], bank(j % 2), OP.add),
                reads=[PSB[j % 2]], writes=[rYA])
        ln_stats(YA, rYA, T2, "rms", 1e-6, 1024)
        r2bc, r2br = bcast(row(1024, T2), R("row1024"), 6, T2)
        for i in range(DT):
            p.emit("dv", lambda e, i=i: e.scalar_tensor_tensor(
                HB[:, i, :], YA[:, i, :], wffn[:, i:i + 1], r2bc,
                OP.mult, OP.mult),
                reads=[rYA, R("wffn")] + r2br, writes=[rHB])
        mlp(rHB, AF.Silu, NFF_A, attn_gate_loader, attn_down_loader, False)
        aswitch(arXA)
        p.dma_r("sp", "dld", 4, lambda e, t0=t0: e.dma_start(
            XA[:, :, :], x0v[:, :, t0:t0 + T2]),
            reads=[R("x0s")], writes=[rXA], war=aclaim(arXA, rXA))
        ln_stats(YA, rYA, T2, "ln", 1e-5, 0)
        l3c, l3cr = bcast(row(0, T2), R("row0"), 6, T2)
        l3m, l3mr = bcast(row(T2, T2), R("row0m"), 7, T2)
        for i in range(DT):
            p.emit("dv", lambda e, i=i: e.tensor_tensor(
                B1[:], YA[:, i, :], l3c, OP.mult),
                reads=[rYA] + l3cr, writes=[rB[0]])
            p.emit("dv", lambda e: e.tensor_tensor(B1[:], B1[:], l3m,
                                                   OP.subtract),
                   reads=[rB[0]] + l3mr, writes=[rB[0]])
            p.emit("dv", lambda e, i=i: e.tensor_tensor(
                XA[:, i, :], XA[:, i, :], B1[:], OP.add),
                reads=[rB[0]], writes=[rXA])
        ln_stats(XA, rXA, T2, "ln", 1e-5, 1024)
        l4c, l4cr = bcast(row(1024, T2), R("row1024"), 6, T2)
        l4m, l4mr = bcast(row(1024 + T2, T2), R("row1024m"), 7, T2)
        for i in range(DT):
            p.emit("dv", lambda e, i=i: e.tensor_tensor(
                B1[:], XA[:, i, :], l4c, OP.mult),
                reads=[rXA] + l4cr, writes=[rB[0]])
            p.emit("dv", lambda e, i=i: e.tensor_tensor(
                HB[:, i, :], B1[:], l4m, OP.subtract),
                reads=[rB[0]] + l4mr, writes=[rHB])
        p.dma_r("sp", "dsc", 4, lambda e, t0=t0: e.dma_start(
            x1v[:, :, t0:t0 + T2], XA[:, :, :]),
            reads=[rXA], writes=[R("x1s")])
        p.dma_r("sp", "dsc", 4, lambda e, t0=t0: e.dma_start(
            h2v[:, :, t0:t0 + T2], HB[:, :, :]),
            reads=[rHB], writes=[R("h2s")])

    run_dq(10 ** 9)
    for ch2 in range(NC2):
        t0 = ch2 * T2
        p.dma_r("sp", "dld", 4, lambda e, t0=t0: e.dma_start(
            HB[:, :, :], h2v[:, :, t0:t0 + T2]),
            reads=[R("h2s")], writes=[rHB])
        mlp(rHB, AF.Gelu_apprx_tanh, NFF_Q, q_gate_loader, q_down_loader, True)
        aswitch(arXA)
        p.dma_r("sp", "dld", 4, lambda e, t0=t0: e.dma_start(
            XA[:, :, :], x1v[:, :, t0:t0 + T2]),
            reads=[R("x1s")], writes=[rXA], war=aclaim(arXA, rXA))
        ln_stats(YA, rYA, T2, "ln", 1e-5, 0)
        l5c, l5cr = bcast(row(0, T2), R("row0"), 6, T2)
        l5m, l5mr = bcast(row(T2, T2), R("row0m"), 7, T2)
        for i in range(DT):
            p.emit("dv", lambda e, i=i: e.tensor_tensor(
                B1[:], YA[:, i, :], l5c, OP.mult),
                reads=[rYA] + l5cr, writes=[rB[0]])
            p.emit("dv", lambda e: e.tensor_tensor(B1[:], B1[:], l5m,
                                                   OP.subtract),
                   reads=[rB[0]] + l5mr, writes=[rB[0]])
            p.emit("dv", lambda e, i=i: e.tensor_tensor(
                XA[:, i, :], XA[:, i, :], B1[:], OP.add),
                reads=[rB[0]], writes=[rXA])
        for tb in range(T2 // 128):
            slot = tb % 2
            for bq in range(4):
                def tfn(e, tb=tb, bq=bq):
                    mm = None
                    for k in range(4):
                        dt_i = bq * 4 + k
                        mm = e.transpose(
                            psum[:, bq * 512 + 128 * k:bq * 512 + 128 * (k + 1)],
                            XA[:, dt_i, tb * 128:(tb + 1) * 128], ident[:])
                    return mm
                p.emit("pe", tfn, reads=[rXA, R("ident")], writes=[PSB[bq]])
                p.emit("act", lambda e, bq=bq, slot=slot: e.activation(
                    XT[:, slot, bq * 512:(bq + 1) * 512],
                    psum[:, bq * 512:(bq + 1) * 512], AF.Copy),
                    reads=[PSB[bq]], writes=[rXT[slot]])
            sl = slice(t0 + tb * 128, t0 + (tb + 1) * 128)
            p.dma_r("sp", "dou", 2, lambda e, sl=sl, slot=slot: e.dma_start(
                y_d[sl, :], XT[:, slot, :]),
                reads=[rXT[slot]], writes=[R("yout")])
    p.wait_all("sp")

    p.replay(nc)
    stack.close()
    return nc


_NC_CACHE = {}


def kernel(L0_attn_x, L0_attn_pre_norm_w, L0_attn_w_a, L0_attn_w_i,
           L0_attn_w_v, L0_attn_out_proj_w, L0_attn_decay_bias,
           L0_attn_ffn_norm_w, L0_attn_w_gate_up, L0_attn_w_down,
           L0_ffn_gate_W_packed, L0_ffn_gate_scales, L0_ffn_up_W_packed,
           L0_ffn_up_scales, L0_ffn_down_W_packed, L0_ffn_down_scales):
    x = np.asarray(L0_attn_x, dtype=np.float32)
    B, S, d = x.shape
    assert d == D
    n_cores = 8
    per_b = n_cores // B
    S4 = S // per_b

    if S4 not in _NC_CACHE:
        _NC_CACHE[S4] = build_program(S4)
    nc = _NC_CACHE[S4]

    def as32(a):
        return np.ascontiguousarray(np.asarray(a), dtype=np.float32)

    def col_tiles(v):
        return np.ascontiguousarray(as32(v).reshape(DT, 128).T)

    common = {
        "w_a": as32(L0_attn_w_a), "w_i": as32(L0_attn_w_i),
        "w_v": as32(L0_attn_w_v), "out_proj_w": as32(L0_attn_out_proj_w),
        "w_gate_up": as32(L0_attn_w_gate_up), "w_down": as32(L0_attn_w_down),
        "gate_W": np.ascontiguousarray(np.asarray(L0_ffn_gate_W_packed,
                                                  dtype=np.int32)),
        "gate_S": as32(L0_ffn_gate_scales),
        "up_W": np.ascontiguousarray(np.asarray(L0_ffn_up_W_packed,
                                                dtype=np.int32)),
        "up_S": as32(L0_ffn_up_scales),
        "down_W": np.ascontiguousarray(np.asarray(L0_ffn_down_W_packed,
                                                  dtype=np.int32)),
        "down_S": as32(L0_ffn_down_scales),
        "dbias": col_tiles(L0_attn_decay_bias),
        "wpre": col_tiles(L0_attn_pre_norm_w),
        "wffn": col_tiles(L0_attn_ffn_norm_w),
        "ident": np.eye(128, dtype=np.float32),
    }
    in_maps = []
    for c in range(n_cores):
        b = c // per_b
        pos = c % per_b
        m = dict(common)
        m["x"] = np.ascontiguousarray(x[b, pos * S4:(pos + 1) * S4, :])
        mk = np.zeros((128, 4), dtype=np.float32)
        mk[:, :pos] = 1.0
        m["mask"] = mk
        in_maps.append(m)

    res = run_bass_kernel_spmd(nc, in_maps, core_ids=list(range(n_cores)))
    out = np.empty((B, S, D), dtype=np.float32)
    for c in range(n_cores):
        b = c // per_b
        pos = c % per_b
        out[b, pos * S4:(pos + 1) * S4, :] = res.results[c]["y"]
    return out


if __name__ == "__main__":
    nc = build_program(4096)
    print("built OK")


# version 09
# speedup vs baseline: 1.0622x; 1.0445x over previous
import contextlib
import numpy as np
import ml_dtypes
import concourse.bass as bass
import concourse.mybir as mybir
from concourse.bass_utils import run_bass_kernel_spmd

F32 = mybir.dt.float32
BF16 = mybir.dt.bfloat16
I32 = mybir.dt.int32
AF = mybir.ActivationFunctionType
OP = mybir.AluOpType
AX = mybir.AxisListType
BFNP = ml_dtypes.bfloat16

D = 2048
DT = D // 128
NFF_A = 8192
NFF_Q = 16384
CH = 64
T1 = 512
T2 = 512
QW_W = 512


class Resrc:
    __slots__ = ("name", "w", "rs")

    def __init__(self, name):
        self.name = name
        self.w = None
        self.rs = []


class Prog:
    COMPUTE = ("pe", "dv", "act", "gp")

    def __init__(self):
        self.plans = {"pe": [], "dv": [], "act": [], "gp": [], "sp": []}
        self.cnt = {}
        self.vc = {k: {} for k in self.plans}
        self.semkeys = []
        self.rotc = {}

    def sem(self, key):
        if key not in self.cnt:
            self.cnt[key] = 0
            self.semkeys.append(key)
        return key

    def _wait(self, ek, semkey, val):
        if val <= 0 or self.vc[ek].get(semkey, 0) >= val:
            return
        self.vc[ek][semkey] = val
        self.plans[ek].append(("wait", semkey, val))

    def _edges(self, ek, reads, writes, war=()):
        need = {}
        for r in reads:
            if r.w is not None:
                k, v = r.w
                need[k] = max(need.get(k, 0), v)
        for w in list(writes) + list(war):
            if w.w is not None:
                k, v = w.w
                need[k] = max(need.get(k, 0), v)
            for (k, v) in w.rs:
                need[k] = max(need.get(k, 0), v)
        for k, v in need.items():
            self._wait(ek, k, v)

    def emit(self, ek, fn, reads=(), writes=(), semkey=None, inc=1, war=()):
        assert ek in self.COMPUTE
        semkey = self.sem(semkey or ek)
        self._edges(ek, reads, writes, war)
        self.cnt[semkey] += inc
        ev = (semkey, self.cnt[semkey])
        self.plans[ek].append(("op", fn, semkey, inc))
        for r in reads:
            r.rs.append(ev)
        for w in writes:
            w.w = ev
            w.rs = []
        return ev

    def dma_r(self, ek, semkey, rot, fn, reads=(), writes=(), war=()):
        return self.dma(ek, semkey, fn, reads=reads, writes=writes, rot=rot,
                        war=war)

    def dma(self, ek, semkey, fn, reads=(), writes=(), rot=1, war=()):
        if rot > 1:
            i = self.rotc.get(semkey, 0)
            self.rotc[semkey] = i + 1
            semkey = f"{semkey}{i % rot}"
        semkey = self.sem(semkey)
        self._edges(ek, reads, writes, war)
        self._wait(ek, semkey, self.cnt[semkey])
        self.cnt[semkey] += 16
        ev = (semkey, self.cnt[semkey])
        self.plans[ek].append(("op", fn, semkey, 16))
        for r in reads:
            r.rs.append(ev)
        for w in writes:
            w.w = ev
            w.rs = []
        return ev

    def wait_all(self, ek):
        for k in list(self.semkeys):
            self._wait(ek, k, self.cnt[k])

    def replay(self, nc):
        with contextlib.ExitStack() as stack:
            sems = {k: stack.enter_context(nc.semaphore(f"S_{k}"))
                    for k in self.semkeys}
            block = stack.enter_context(nc.Block())
            for ek, attr in (("sp", "sync"), ("pe", "tensor"),
                             ("dv", "vector"), ("act", "scalar"),
                             ("gp", "gpsimd")):
                plan = self.plans[ek]

                def body(eng, plan=plan):
                    for item in plan:
                        if item[0] == "wait":
                            eng.wait_ge(sems[item[1]], item[2])
                        else:
                            _, fn, semkey, inc = item
                            instr = fn(eng)
                            assert instr is not None
                            instr.then_inc(sems[semkey], inc)

                getattr(block, attr)(body)


def build_program(S4):
    assert S4 % T1 == 0 and S4 % T2 == 0 and T1 % CH == 0 and T2 % CH == 0
    NC1 = S4 // T1
    NC2 = S4 // T2
    CHN1 = T1 // CH
    CHN2 = T2 // CH
    nc = bass.Bass("TRN2", target_bir_lowering=False, debug=False,
                   disable_frame_to_traceback=True)
    p = Prog()
    res = {}

    def R(name):
        if name not in res:
            res[name] = Resrc(name)
        return res[name]

    def din(name, shape, dt=F32):
        return nc.dram_tensor(name, shape, dt, kind="ExternalInput").ap()

    x_d = din("x", [S4, D])
    wa_d = din("w_a", [D, D])
    wi_d = din("w_i", [D, D])
    wv_d = din("w_v", [D, D])
    op_d = din("out_proj_w", [D, D])
    wgu_d = din("w_gate_up", [D, 2 * NFF_A])
    wd_d = din("w_down", [NFF_A, D])
    gW_d = din("gate_W", [NFF_Q, D // 2], I32)
    gS_d = din("gate_S", [NFF_Q, D // 32])
    uW_d = din("up_W", [NFF_Q, D // 2], I32)
    uS_d = din("up_S", [NFF_Q, D // 32])
    dW_d = din("down_W", [D, NFF_Q // 2], I32)
    dS_d = din("down_S", [D, NFF_Q // 32])
    dbias_d = din("dbias", [128, DT])
    wpre_d = din("wpre", [128, DT])
    wffn_d = din("wffn", [128, DT])
    ident_d = din("ident", [128, 128])
    mask_d = din("mask", [128, 4])
    y_d = nc.dram_tensor("y", [S4, D], F32, kind="ExternalOutput").ap()

    def scr(name, shape, dt):
        return nc.dram_tensor(name, shape, dt, kind="Internal").ap()

    wa_blk = scr("wa_blk", [DT, 128, DT, 128], BF16)
    wi_blk = scr("wi_blk", [DT, 128, DT, 128], BF16)
    wv_blk = scr("wv_blk", [DT, 128, DT, 128], BF16)
    op_blk = scr("op_blk", [DT, 128, DT, 128], BF16)
    wgu_blk = scr("wgu_blk", [32, 128, DT, 512], BF16)
    wd_bf = scr("wd_bf", [NFF_A, D], BF16)
    dqgT = scr("dqgT", [32, 128, DT, 512], BF16)
    dquT = scr("dquT", [32, 128, DT, 512], BF16)
    dqdT = scr("dqdT", [128, 128, D], BF16)
    cd_scr = scr("cd_scr", [DT, 128, S4], F32)
    in_scr = scr("in_scr", [DT, 128, S4], F32)
    x0_scr = scr("x0_scr", [DT, 128, S4], F32)
    x1_scr = scr("x1_scr", [DT, 128, S4], F32)
    h2_scr = scr("h2_scr", [DT, 128, S4], BF16)
    c1r_scr = scr("c1r_scr", [1, S4], F32)
    m1r_scr = scr("m1r_scr", [1, S4], F32)
    cc_in = scr("cc_in", [128, 32], F32)
    cc_out = scr("cc_out", [512, 32], F32)

    cdv = cd_scr.rearrange("a p s -> p a s")
    inv = in_scr.rearrange("a p s -> p a s")
    x0v = x0_scr.rearrange("a p s -> p a s")
    x1v = x1_scr.rearrange("a p s -> p a s")
    h2v = h2_scr.rearrange("a p s -> p a s")

    stack = contextlib.ExitStack()

    def sb(name, shape, dt):
        return stack.enter_context(nc.sbuf_tensor(name, shape, dt))

    ident = sb("identS", [128, 128], F32)
    ones1 = sb("ones1", [1, 128], F32)
    onesb = sb("onesb", [128, 1], BF16)
    dbias = sb("dbiasS", [128, DT], F32)
    wpre = sb("wpreS", [128, DT], F32)
    wffn = sb("wffnS", [128, DT], F32)
    maskT = sb("maskT", [128, 4], F32)
    carry = sb("carry", [128, DT], F32)
    ctmp = sb("ctmp", [128, DT], F32)
    c0cur = sb("c0cur", [128, DT], F32)
    sumla = sb("sumla", [128, DT], F32)
    cfQ = sb("cfQ", [128, 32], F32)
    ccg = sb("ccg", [128, 4, 32], F32)
    CS2 = sb("CS2", [128, DT, CHN2 + 1], F32)
    rows = sb("rows", [1, 2560], F32)
    XT = sb("XT", [128, 2, D], F32)
    XA = sb("XA", [128, DT, T2], F32)
    YA = sb("YA", [128, DT, T2], F32)
    CDB = sb("CDB", [128, DT, T2], F32)
    INB = sb("INB", [128, DT, T2], F32)
    HB = sb("HB", [128, DT, T2], BF16)
    IG = sb("IG", [128, 8, T2], BF16)
    IGf = IG[:].rearrange("p a b -> p (a b)").bitcast(F32)
    B2 = IGf[:, 0:T1]
    B3 = IGf[:, T1:2 * T1]
    B4s = [IGf[:, 2 * T1:3 * T1], IGf[:, 3 * T1:4 * T1]]
    B1t = sb("B1", [128, 2, T1], F32)
    B1 = B1t[:, 0, :]
    IGf = None
    B5 = sb("B5", [128, 16], F32)
    DMP = sb("DMP", [128, 2, T2], BF16)
    ones32 = sb("ones32", [128, 1], F32)
    identb = sb("identB", [128, 128], BF16)
    QWt = sb("QWt", [128, 2, QW_W], I32)
    QSC = sb("QSC", [128, 2, QW_W // 16], F32)
    QAB = sb("QAB", [128, QW_W], I32)
    QC = sb("QC", [128, 2, QW_W], I32)
    QD = sb("QD", [128, QW_W], F32)
    QF = sb("QF", [128, QW_W], F32)
    QO = sb("QOt", [128, 2, 2 * QW_W], BF16)
    QT = sb("QTt", [128, 1, 2 * QW_W], BF16)

    psum = stack.enter_context(nc.psum_tensor("ps", [128, 4096], F32))
    PSB = [R(f"psb{i}") for i in range(8)]

    def bank(i):
        return psum[:, 512 * i:512 * (i + 1)]

    CDW = CDB[:].rearrange("p a b -> p (a b)").bitcast(BF16).rearrange("p (a b) -> p a b", a=DT)
    INW = INB[:].rearrange("p a b -> p (a b)").bitcast(BF16).rearrange("p (a b) -> p a b", a=DT)
    INW8 = INB[:].rearrange("p a b -> p (a b)").bitcast(BF16).rearrange("p (a b) -> p a b", a=8)

    rXA, rYA, rCDB, rINB, rHB, rIG = (R("XA"), R("YA"), R("CDB"), R("INB"),
                                      R("HB"), R("IG"))

    class Ar:
        def __init__(self):
            self.cur = set()
            self.prev = []

    arCDB, arINB, arIG, arXA = Ar(), Ar(), Ar(), Ar()

    def aclaim(ar, r):
        if r in ar.cur:
            return []
        ar.cur.add(r)
        return ar.prev

    def aswitch(ar):
        if ar.cur:
            ar.prev = list(ar.cur)
            ar.cur = set()
    rXT = [R("XT0"), R("XT1")]
    rB1 = [R("B1_0"), R("B1_1")]
    rB4s = [R("B4_0"), R("B4_1")]
    rB = [rB1[0], R("B2"), R("B3"), rB4s[0], R("B5")]
    rDMP = R("DMP")
    XAW = (XA[:].rearrange("p a b -> p (a b)").bitcast(BF16)
           .rearrange("p (a b) -> p a b", a=DT))

    for t, d_src, rr in ((ident, ident_d, R("ident")),
                         (dbias, dbias_d, R("dbias")),
                         (wpre, wpre_d, R("wpre")), (wffn, wffn_d, R("wffn")),
                         (maskT, mask_d, R("maskT"))):
        p.dma_r("sp", "dxa", 2, lambda e, t=t, s=d_src: e.dma_start(t[:], s),
              writes=[rr])
    p.emit("act", lambda e: e.activation(identb[:], ident[:], AF.Copy),
           reads=[R("ident")], writes=[R("identb")])
    p.emit("dv", lambda e: e.memset(ones1[:], 1.0), writes=[R("ones1")])
    p.emit("dv", lambda e: e.memset(onesb[:], 1.0), writes=[R("onesb")])
    p.emit("dv", lambda e: e.memset(ones32[:], 1.0), writes=[R("ones32")])
    p.emit("dv", lambda e: e.memset(c0cur[:], 0.0), writes=[R("c0cur")])
    p.emit("dv", lambda e: e.memset(sumla[:], 0.0), writes=[R("sumla")])
    for cval in (1e-5, 1e-6):
        cten = sb(f"cst{cval:.0e}", [128, 1], F32)
        nc.const_aps.register(F32, cval, cten.ap())
        p.emit("dv", lambda e, cten=cten, cval=cval: e.memset(cten[:], cval),
               writes=[R("consts")])
    _initev = ("dv", p.cnt.get("dv", 0))
    for _ek in ("pe", "act", "gp", "sp"):
        p._wait(_ek, "dv", _initev[1])

    for i in range(4):
        sl = slice(i * (NFF_A // 4), (i + 1) * (NFF_A // 4))
        p.dma_r("gp", "gcv", 2,
                lambda e, sl=sl: e.dma_start(wd_bf[sl, :], wd_d[sl, :]),
                writes=[R("cv_wd")])
    CONV = CDW
    for nm, (w_src, blkt) in {"wa": (wa_d, wa_blk), "wi": (wi_d, wi_blk),
                              "wv": (wv_d, wv_blk), "op": (op_d, op_blk)}.items():
        src3f = w_src.rearrange("(a p) n -> p a n", p=128)
        for j in range(DT):
            s = j % 8
            rcv = R(f"cvb{s}")
            p.dma_r("gp", "gcv", 2,
                    lambda e, j=j, s=s, src3f=src3f: e.dma_start(
                        CONV[:, :, 128 * s:128 * (s + 1)],
                        src3f[:, :, j * 128:(j + 1) * 128]),
                    writes=[rcv], war=aclaim(arCDB, rcv))
            p.dma_r("sp", "gcs", 2,
                    lambda e, j=j, s=s, blkt=blkt: e.dma_start(
                        blkt[j].rearrange("p a n -> p (a n)")
                        .rearrange("p (a n) -> p a n", a=DT),
                        CONV[:, :, 128 * s:128 * (s + 1)]),
                    reads=[rcv], writes=[R(f"cv_{nm}")])
    aswitch(arCDB)
    wgu3f = wgu_d.rearrange("(a p) n -> p a n", p=128)
    for lg in range(32):
        col0 = lg * 512
        half = lg % 2
        rcv = R(f"cvg{half}")
        p.dma_r("gp", "gcv", 2,
                lambda e, col0=col0, half=half: e.dma_start(
                    CDW[:, :, 512 * half:512 * (half + 1)],
                    wgu3f[:, :, col0:col0 + 512]),
                writes=[rcv], war=aclaim(arCDB, rcv))
        p.dma_r("sp", "gcs", 2,
                lambda e, lg=lg, half=half: e.dma_start(
                    wgu_blk[lg], CDW[:, :, 512 * half:512 * (half + 1)]),
                reads=[rcv], writes=[R("cv_wgu")])

    dq_queue = []
    rdq = {"g": R("dq_g"), "u": R("dq_u"), "d": R("dq_d")}
    psumb = psum[:].rearrange("p (a b) -> p a b", a=8).bitcast(BF16)

    def dequant_tile(W_src, S_src, dst_fn, r0, c0, slot, rtag):
        qw = QWt[:, slot, :]
        qsc = QSC[:, slot, :]
        qo = QO[:, slot, :]
        qt = QT[:, 0, :]
        dqb = 6 + slot
        rqw, rqs = R(f"qw{slot}"), R(f"qs{slot}")
        rqo, rqt = R(f"qo{slot}"), R("qt0")
        a = b = QAB[:]
        d, f = QD[:], QF[:]
        ra = rb = R("q_ab")
        rd, rf = R("q_d"), R("q_f")

        def plan():
            p.dma_r("sp", "gdq", 2,
                  lambda e: e.dma_start(qw, W_src[r0:r0 + 128, c0:c0 + QW_W]),
                  writes=[rqw])
            p.dma_r("sp", "gdq", 2,
                  lambda e: e.dma_start(
                      qsc, S_src[r0:r0 + 128, c0 // 16:(c0 + QW_W) // 16]),
                  writes=[rqs])
            p.emit("dv", lambda e: e.tensor_copy(
                f.rearrange("p (k r) -> p k r", r=16),
                qsc.unsqueeze(2).broadcast_to([128, QW_W // 16, 16])),
                reads=[rqs], writes=[rf])
            for half in range(2):
                c = QC[:, slot, :]
                rc = R(f"qc{slot}")
                if half == 0:
                    p.emit("dv", lambda e: e.tensor_scalar(
                        a, qw, 7, None, OP.bitwise_and),
                        reads=[rqw], writes=[ra])
                else:
                    p.emit("dv", lambda e: e.tensor_scalar(
                        a, qw, 4, 7, OP.logical_shift_right, OP.bitwise_and),
                        reads=[rqw], writes=[ra])
                p.emit("gp", lambda e, c=c: e.tensor_scalar(
                    c, a, float(1 << 22), float(0x3F000000), OP.mult, OP.add),
                    reads=[ra], writes=[rc])
                p.emit("dv", lambda e: e.tensor_scalar(
                    d, a, -0.25, 0.5, OP.mult, OP.add),
                    reads=[ra], writes=[rd])
                p.emit("act", lambda e: e.activation(d, d, AF.Relu),
                       reads=[rd], writes=[rd])
                eng_s = "dv" if half == 0 else "gp"
                p.emit(eng_s, lambda e, c=c: e.tensor_tensor(
                    c.bitcast(F32), c.bitcast(F32), d, OP.subtract),
                    reads=[rc, rd], writes=[rc])
                if half == 0:
                    p.emit("dv", lambda e: e.tensor_scalar(
                        b, qw, 8, 28, OP.bitwise_and, OP.logical_shift_left),
                        reads=[rqw], writes=[rb])
                else:
                    p.emit("dv", lambda e: e.tensor_scalar(
                        b, qw, 128, 24, OP.bitwise_and, OP.logical_shift_left),
                        reads=[rqw], writes=[rb])
                p.emit("dv", lambda e, c=c: e.tensor_tensor(
                    c, c, b, OP.bitwise_or),
                    reads=[rc, rb], writes=[rc])
                qo_h = qo.rearrange("p (k two) -> p k two", two=2)[:, :, half]
                p.emit("dv", lambda e, c=c, qo_h=qo_h: e.tensor_tensor(
                    qo_h, c.bitcast(F32), f, OP.mult),
                    reads=[rc, rf], writes=[rqo])

            def tfn(e):
                mm = None
                for t in range(8):
                    mm = e.transpose(
                        psumb[:, dqb, 128 * t:128 * (t + 1)],
                        qo[:, 128 * t:128 * (t + 1)], identb[:])
                return mm
            p.emit("pe", tfn, reads=[rqo, R("identb")], writes=[PSB[dqb]])
            p.emit("act", lambda e: e.activation(qt, psumb[:, dqb, :],
                                                 AF.Copy),
                   reads=[PSB[dqb]], writes=[rqt])
            p.dma_r("act", "gds", 2,
                  lambda e: e.dma_start(dst_fn(), qt.rearrange(
                      "p (a n) -> p a n", a=8)),
                  reads=[rqt], writes=[rtag])
        dq_queue.append(plan)

    dqdTv = dqdT.rearrange("j p d -> p j d")
    sl_i = [0]
    for (Wsrc, Ssrc, dqT, nrows, words, key) in (
            (gW_d, gS_d, dqgT, NFF_Q, D // 2, "g"),
            (uW_d, uS_d, dquT, NFF_Q, D // 2, "u"),
            (dW_d, dS_d, None, D, NFF_Q // 2, "d")):
        for r0 in range(0, nrows, 128):
            for c0 in range(0, words, QW_W):
                if key == "d":
                    j0 = (2 * c0) // 128
                    dst_fn = (lambda j0=j0, r0=r0:
                              dqdTv[:, j0:j0 + 8, r0:r0 + 128])
                else:
                    lg, n0 = r0 // 512, r0 % 512
                    hc0 = (2 * c0) // 128
                    dst_fn = (lambda dqT=dqT, lg=lg, hc0=hc0, n0=n0:
                              dqT[lg, :, hc0:hc0 + 8, n0:n0 + 128])
                dequant_tile(Wsrc, Ssrc, dst_fn, r0, c0, sl_i[0] % 2,
                             rdq[key])
                sl_i[0] += 1
    DQ_TOTAL = len(dq_queue)
    dq_done = [0]

    def run_dq(n):
        while n > 0 and dq_done[0] < DQ_TOTAL:
            dq_queue[dq_done[0]]()
            dq_done[0] += 1
            n -= 1

    def mmgrp(out_ap, pairs, bank_res_list, extra_reads, first=True,
              last=True):
        def fn(e):
            mm = None
            n = len(pairs)
            for i, (lt, rt) in enumerate(pairs):
                mm = e.matmul(out_ap, lt, rt, start=(first and i == 0),
                              stop=(last and i == n - 1))
            return mm
        return p.emit("pe", fn, reads=list(extra_reads), writes=bank_res_list)

    def row(off, t):
        return rows[0:1, off:off + t]

    rDMP2 = [R("DMP0"), R("DMP1")]

    def ln_stats(arena, arena_res, t_w, kind, eps, off):
        want_mean = kind != "rms"
        rsumv = psum[0:1, 6 * 512:6 * 512 + t_w]
        rsqv = psum[0:1, 7 * 512:7 * 512 + t_w]
        for i in range(DT):
            dsl = DMP[:, i % 2, 0:t_w]
            rdm = rDMP2[i % 2]
            p.emit("act", lambda e, i=i, dsl=dsl: e.activation(
                dsl, arena[:, i, 0:t_w], AF.Square),
                reads=[arena_res], writes=[rdm])
            mmgrp(rsqv, [(onesb[:, :], dsl)], [PSB[7]],
                  [rdm, R("onesb")], first=(i == 0), last=(i == DT - 1))
            if want_mean:
                mmgrp(rsumv, [(ones32[:, :], arena[:, i, 0:t_w])], [PSB[6]],
                      [arena_res, R("ones32")], first=(i == 0),
                      last=(i == DT - 1))
        if kind == "rms":
            ro = row(off, t_w)
            rr = R(f"row{off}")
            p.emit("act", lambda e: e.activation(
                ro, rsqv, AF.Ln, bias=float(eps), scale=1.0 / D),
                reads=[PSB[7]], writes=[rr])
            p.emit("act", lambda e: e.activation(ro, ro, AF.Exp, scale=-0.5),
                   reads=[rr], writes=[rr])
            return
        if kind == "ln1":
            mu = row(1536, t_w)
            var = row(2048, t_w)
            mu_war = [R("row1024m")]
            var_war = [R("rowmu"), R("rowMS"), R("rowT")]
        else:
            mu = row(2048, t_w)
            voff = 1024 if off == 0 else 512
            var = row(voff, t_w)
            mu_war = [R("rowMS"), R("rowT"), R("rowvar")]
            var_war = [R(f"row{voff}"), R(f"row{voff}m"), R("rowS"),
                       R("row0m")] if voff == 512 else [R(f"row{voff}"),
                                                        R("rowS")]
        rmu, rvar = R("rowmu"), R("rowvar")
        p.emit("act", lambda e: e.activation(mu, rsumv, AF.Copy,
                                             scale=1.0 / D),
               reads=[PSB[6]], writes=[rmu], war=mu_war)
        p.emit("act", lambda e: e.activation(var, mu, AF.Square),
               reads=[rmu], writes=[rvar], war=var_war)
        p.emit("dv", lambda e: e.scalar_tensor_tensor(
            var, rsqv, 1.0 / D, var, OP.mult, OP.subtract),
            reads=[PSB[7], rvar], writes=[rvar])
        co = row(off, t_w)
        mo = row(off + t_w, t_w)
        rco, rmo = R(f"row{off}"), R(f"row{off}m")
        if kind == "ln":
            p.emit("act", lambda e: e.activation(co, var, AF.Ln,
                                                 bias=float(eps)),
                   reads=[rvar], writes=[rco])
            p.emit("act", lambda e: e.activation(co, co, AF.Exp, scale=-0.5),
                   reads=[rco], writes=[rco])
        else:
            s = row(1024, t_w)
            t = row(2048, t_w)
            ms = row(2048, t_w)
            rs_, rt_, rms_ = R("rowS"), R("rowT"), R("rowMS")
            p.emit("act", lambda e: e.activation(s, var, AF.Ln, bias=1e-5),
                   reads=[rvar], writes=[rs_])
            p.emit("act", lambda e: e.activation(s, s, AF.Exp, scale=-0.5),
                   reads=[rs_], writes=[rs_])
            p.emit("dv", lambda e: e.tensor_tensor(t, var, s, OP.mult),
                   reads=[rvar, rs_], writes=[rt_])
            p.emit("dv", lambda e: e.tensor_tensor(t, t, s, OP.mult),
                   reads=[rt_, rs_], writes=[rt_])
            p.emit("act", lambda e: e.activation(t, t, AF.Ln, bias=1e-6),
                   reads=[rt_], writes=[rt_])
            p.emit("act", lambda e: e.activation(t, t, AF.Exp, scale=-0.5),
                   reads=[rt_], writes=[rt_])
            p.emit("dv", lambda e: e.tensor_tensor(co, s, t, OP.mult),
                   reads=[rs_, rt_], writes=[rco])
            p.emit("dv", lambda e: e.tensor_tensor(ms, mu, s, OP.mult),
                   reads=[rmu, rs_], writes=[rms_], war=[rt_])
        p.emit("dv", lambda e: e.tensor_tensor(mo, mu, co, OP.mult),
               reads=[rmu, rco], writes=[rmo])

    def bcast(row_ap, row_res, bank_i, t_w):
        nb = (t_w + 511) // 512
        outs = [PSB[bank_i + i] for i in range(nb)]

        def fn(e):
            mm = None
            for i in range(nb):
                w = min(512, t_w - i * 512)
                mm = e.matmul(
                    psum[:, 512 * (bank_i + i):512 * (bank_i + i) + w],
                    ones1[:, :], row_ap[:, i * 512:i * 512 + w],
                    start=True, stop=True)
            return mm
        p.emit("pe", fn, reads=[row_res, R("ones1")], writes=outs)
        return psum[:, 512 * bank_i:512 * bank_i + t_w], outs


    def wslot(s):
        return CDW.rearrange("p a (s c) -> p a s c", c=128)[:, :, s, :]

    aswitch(arCDB)
    for ch1 in range(NC1):
        t0 = ch1 * T1
        for tb in range(T1 // 128):
            sl = slice(t0 + tb * 128, t0 + (tb + 1) * 128)
            slot = tb % 2
            p.dma_r("sp", "dxa", 2,
                  lambda e, sl=sl, slot=slot: e.dma_start(XT[:, slot, :],
                                                          x_d[sl, :]),
                  writes=[rXT[slot]])
            for bq in range(4):
                def tfn(e, slot=slot, bq=bq):
                    mm = None
                    for k in range(4):
                        dt_i = bq * 4 + k
                        mm = e.transpose(
                            psum[:, bq * 512 + 128 * k:bq * 512 + 128 * (k + 1)],
                            XT[:, slot, 128 * dt_i:128 * (dt_i + 1)], ident[:])
                    return mm
                p.emit("pe", tfn, reads=[rXT[slot], R("ident")],
                       writes=[PSB[bq]])
                p.emit("act", lambda e, bq=bq, tb=tb: e.activation(
                    XA[:, bq * 4:(bq + 1) * 4, tb * 128:(tb + 1) * 128],
                    psum[:, bq * 512:(bq + 1) * 512]
                    .rearrange("p (a b) -> p a b", a=4), AF.Copy),
                    reads=[PSB[bq]], writes=[rXA])
        p.dma_r("sp", "dsc", 4,
              lambda e, t0=t0: e.dma_start(x0v[:, :, t0:t0 + T1],
                                           XA[:, :, 0:T1]),
              reads=[rXA], writes=[R("x0s")])
        ln_stats(XA, rXA, T1, "ln1", None, 0)
        p.dma_r("sp", "dsc", 4, lambda e, t0=t0: e.dma_start(
            c1r_scr[0:1, t0:t0 + T1], row(1024, T1)),
            reads=[R("rowS")], writes=[R("c1rs")])
        p.dma_r("sp", "dsc", 4, lambda e, t0=t0: e.dma_start(
            m1r_scr[0:1, t0:t0 + T1], row(2048, T1)),
            reads=[R("rowMS")], writes=[R("m1rs")])
        cbc, cbr = bcast(row(0, T1), R("row0"), 6, T1)
        mbc, mbr = bcast(row(T1, T1), R("row0m"), 7, T1)
        p.emit("dv", lambda e: e.tensor_tensor(
            YA[:, :, 0:T1], XA[:, :, 0:T1],
            cbc.unsqueeze(1).broadcast_to([128, DT, T1]), OP.mult),
            reads=[rXA] + cbr, writes=[rYA])
        p.emit("dv", lambda e: e.tensor_tensor(
            YA[:, :, 0:T1], YA[:, :, 0:T1],
            mbc.unsqueeze(1).broadcast_to([128, DT, T1]), OP.subtract),
            reads=[rYA] + mbr, writes=[rYA])
        for i in range(DT):
            p.emit("dv", lambda e, i=i: e.tensor_scalar(
                HB[:, i, 0:T1], YA[:, i, 0:T1], wpre[:, i:i + 1], None,
                OP.mult),
                reads=[rYA, R("wpre")], writes=[rHB])
        for j in range(DT):
            s0 = (j % 2) * 3
            rws = [R(f"ws{s0 + k}") for k in range(3)]
            for k, (wblk, rw, cv) in enumerate(
                    ((wa_blk, rws[0], "cv_wa"), (wi_blk, rws[1], "cv_wi"),
                     (wv_blk, rws[2], "cv_wv"))):
                p.dma_r("gp", "dwa", 4,
                      lambda e, wblk=wblk, j=j, s=s0 + k: e.dma_start(
                          wslot(s), wblk[j]),
                      reads=[R(cv)], writes=[rw], war=aclaim(arCDB, rw))
            ba, bi, bv = (j % 2), 2 + (j % 2), 4 + (j % 2)
            for (s, rw, bk) in ((s0, rws[0], ba), (s0 + 1, rws[1], bi),
                                (s0 + 2, rws[2], bv)):
                pairs = [(wslot(s)[:, kc, :], HB[:, kc, 0:T1])
                         for kc in range(DT)]
                mmgrp(bank(bk), pairs, [PSB[bk]], [rw, rHB])
            Bj = B1t[:, j % 2, :]
            rBj = rB1[j % 2]
            B4, B2b = B4s[j % 2], B4s[1 - j % 2]
            rB4, rB2b = rB4s[j % 2], rB4s[1 - j % 2]
            p.emit("act", lambda e, ba=ba, j=j, Bj=Bj: e.activation(
                Bj, bank(ba), AF.Sigmoid, bias=dbias[:, j:j + 1]),
                reads=[PSB[ba], R("dbias")], writes=[rBj])
            p.emit("act", lambda e, bi=bi: e.activation(
                B2, bank(bi), AF.Sigmoid),
                reads=[PSB[bi]], writes=[rB[1]], war=aclaim(arIG, rB[1]))
            p.emit("gp", lambda e, Bj=Bj: e.tensor_tensor(
                B3, Bj, Bj, OP.mult),
                reads=[rBj], writes=[rB[2]], war=aclaim(arIG, rB[2]))
            p.emit("act", lambda e: e.activation(B3, B3, AF.Ln,
                                                 bias=1.0, scale=-1.0),
                   reads=[rB[2]], writes=[rB[2]])
            p.emit("act", lambda e: e.activation(B3, B3, AF.Exp, scale=0.5),
                   reads=[rB[2]], writes=[rB[2]])
            p.emit("dv", lambda e, bv=bv, B4=B4: e.tensor_tensor(
                B4, B2, bank(bv), OP.mult),
                reads=[rB[1], PSB[bv]], writes=[rB4],
                war=aclaim(arIG, rB4))
            p.emit("gp", lambda e, B4=B4: e.tensor_tensor(B4, B3, B4,
                                                          OP.mult),
                   reads=[rB[2], rB4], writes=[rB4])
            p.emit("act", lambda e, Bj=Bj, B2b=B2b: e.activation(
                B2b, Bj, AF.Ln),
                reads=[rBj], writes=[rB2b], war=aclaim(arIG, rB2b))
            for cc in range(CHN1):
                sl = slice(cc * CH, (cc + 1) * CH)
                p.emit("dv", lambda e, sl=sl, B2b=B2b: e.tensor_tensor_scan(
                    B2b[:, sl], B2b[:, sl], B2b[:, sl], 0.0, OP.add,
                    OP.bypass),
                    reads=[rB2b], writes=[rB2b])
            p.emit("dv", lambda e, B2b=B2b: e.reduce_sum(
                B5[:, 0:1], B2b[:, CH - 1::CH], AX.X),
                reads=[rB2b], writes=[rB[4]])
            p.emit("gp", lambda e, j=j: e.tensor_tensor(
                sumla[:, j:j + 1], sumla[:, j:j + 1], B5[:, 0:1], OP.add),
                reads=[rB[4]], writes=[R("sumla")])
            p.emit("act", lambda e, Bj=Bj, B2b=B2b: e.activation(
                Bj, B2b, AF.Exp),
                reads=[rB2b], writes=[rBj])
            p.emit("dv", lambda e, B2b=B2b: e.tensor_scalar(
                B2b, B2b, -23.025850929940457, None, OP.max),
                reads=[rB2b], writes=[rB2b])
            p.emit("act", lambda e, B2b=B2b: e.activation(
                B2b, B2b, AF.Exp, scale=-1.0),
                reads=[rB2b], writes=[rB2b])
            p.emit("dv", lambda e, B4=B4, B2b=B2b: e.tensor_tensor(
                B4, B4, B2b, OP.mult),
                reads=[rB4, rB2b], writes=[rB4])
            for cc in range(CHN1):
                sl = slice(cc * CH, (cc + 1) * CH)
                p.emit("dv", lambda e, sl=sl, B4=B4: e.tensor_tensor_scan(
                    B4[:, sl], B4[:, sl], B4[:, sl], 0.0, OP.add, OP.bypass),
                    reads=[rB4], writes=[rB4])
            p.emit("dv", lambda e, Bj=Bj, B4=B4: e.tensor_tensor(
                B4, Bj, B4, OP.mult),
                reads=[rBj, rB4], writes=[rB4])
            p.emit("dv", lambda e, j=j, Bj=Bj, B4=B4: e.tensor_tensor_scan(
                B5[:, 1:1 + CHN1], Bj[:, CH - 1::CH], B4[:, CH - 1::CH],
                c0cur[:, j:j + 1], OP.mult, OP.add),
                reads=[rBj, rB4, R("c0cur")], writes=[rB[4]])
            p.emit("gp", lambda e, j=j: e.tensor_copy(
                c0cur[:, j:j + 1], B5[:, CHN1:CHN1 + 1]),
                reads=[rB[4]], writes=[R("c0cur")])
            p.dma_r("sp", "dsc", 4,
                  lambda e, j=j, t0=t0, Bj=Bj: e.dma_start(
                      cdv[:, j, t0:t0 + T1], Bj),
                  reads=[rBj], writes=[R("cds")])
            p.dma_r("sp", "dsc", 4,
                  lambda e, j=j, t0=t0, B4=B4: e.dma_start(
                      inv[:, j, t0:t0 + T1], B4),
                  reads=[rB4], writes=[R("ins")])
            run_dq(2)

    p.emit("dv", lambda e: e.tensor_copy(cfQ[:, 0:DT], c0cur[:]),
           reads=[R("c0cur")], writes=[R("cfQ")])
    p.emit("act", lambda e: e.activation(cfQ[:, DT:2 * DT], sumla[:], AF.Exp),
           reads=[R("sumla")], writes=[R("cfQ")])
    p.dma("sp", "dcc", lambda e: e.dma_start(cc_in[:, :], cfQ[:]),
          reads=[R("cfQ")], writes=[R("ccin")])
    p.emit("gp", lambda e: e.collective_compute(
        "AllGather", OP.bypass, replica_groups=[[0, 1, 2, 3], [4, 5, 6, 7]],
        ins=[cc_in.opt()], outs=[cc_out.opt()]),
        reads=[R("ccin")], writes=[R("ccout")], semkey="ccs", inc=1)
    ccov = cc_out.rearrange("(r p) c -> p r c", p=128)
    p.dma("sp", "dcc", lambda e: e.dma_start(ccg[:], ccov),
          reads=[R("ccout")], writes=[R("ccg")])
    p.emit("dv", lambda e: e.memset(carry[:], 0.0), writes=[R("carry")])
    for r in range(4):
        p.emit("dv", lambda e, r=r: e.tensor_tensor(
            ctmp[:], carry[:], ccg[:, r, DT:2 * DT], OP.mult),
            reads=[R("carry"), R("ccg")], writes=[R("ctmp")])
        p.emit("dv", lambda e, r=r: e.tensor_tensor(
            ctmp[:], ctmp[:], ccg[:, r, 0:DT], OP.add),
            reads=[R("ctmp"), R("ccg")], writes=[R("ctmp")])
        p.emit("dv", lambda e: e.tensor_tensor(
            ctmp[:], ctmp[:], carry[:], OP.subtract),
            reads=[R("ctmp"), R("carry")], writes=[R("ctmp")])
        p.emit("dv", lambda e, r=r: e.scalar_tensor_tensor(
            carry[:], ctmp[:], maskT[:, r:r + 1], carry[:], OP.mult, OP.add),
            reads=[R("ctmp"), R("maskT")], writes=[R("carry")])

    def mlp(rhs_res, act_f, nff, gate_loader, down_loader, first_copy):
        aswitch(arCDB)
        aswitch(arINB)
        aswitch(arXA)
        aswitch(arIG)
        n_sg = nff // 1024
        for sg in range(n_sg):
            for lg in range(2):
                par = (sg * 2 + lg) % 2
                W = CDW if par == 0 else XAW
                ar = arCDB if par == 0 else arXA
                jj0 = sg * 8 + lg * 4
                rg, ru = R(f"wgt_g{par}"), R(f"wgt_u{par}")
                gate_loader(jj0, W, ar, rg, ru)
                for k in range(4):
                    jj = lg * 4 + k
                    bg, bu = (jj % 2), 2 + (jj % 2)
                    gp_ = [(W[:, kc, 128 * k:128 * (k + 1)], HB[:, kc, :])
                           for kc in range(DT)]
                    up_ = [(W[:, kc, 512 + 128 * k:512 + 128 * (k + 1)],
                            HB[:, kc, :]) for kc in range(DT)]
                    mmgrp(bank(bg), gp_, [PSB[bg]], [rg, rhs_res])
                    mmgrp(bank(bu), up_, [PSB[bu]], [ru, rhs_res])
                    p.emit("act", lambda e, bg=bg: e.activation(
                        B1[:, 0:T2], bank(bg), act_f),
                        reads=[PSB[bg]], writes=[rB[0]])
                    p.emit("dv", lambda e, bu=bu, jj=jj: e.tensor_tensor(
                        IG[:, jj, :], B1[:, 0:T2], bank(bu), OP.mult),
                        reads=[rB[0], PSB[bu]], writes=[rIG],
                        war=aclaim(arIG, rIG))
            rwd = [R(f"wd{s}") for s in range(8)]
            for s in range(8):
                down_loader(s, sg * 8 + s, rwd[s])
            for q in range(4):
                def dfn(e, q=q):
                    mm = None
                    for jj in range(8):
                        for k in range(4):
                            dcc = q * 4 + k
                            mm = e.matmul(
                                bank(4 + k),
                                INW8[:, jj, 128 * dcc:128 * (dcc + 1)],
                                IG[:, jj, :],
                                start=(jj == 0), stop=(jj == 7))
                    return mm
                p.emit("pe", dfn, reads=rwd + [rIG],
                       writes=[PSB[4 + i] for i in range(4)])
                dst = YA[:, q * 4:(q + 1) * 4, :]
                srcv = psum[:, 4 * 512:8 * 512].rearrange("p (a b) -> p a b",
                                                          a=4)
                if first_copy and sg == 0:
                    p.emit("dv", lambda e, dst=dst, srcv=srcv: e.tensor_copy(
                        dst, srcv),
                        reads=[PSB[4 + i] for i in range(4)], writes=[rYA])
                else:
                    p.emit("dv", lambda e, dst=dst, srcv=srcv: e.tensor_tensor(
                        dst, dst, srcv, OP.add),
                        reads=[PSB[4 + i] for i in range(4)], writes=[rYA])
            run_dq(8)

    def attn_gate_loader(jj0, W, ar, rg, ru):
        p.dma_r("gp", "dwg", 4, lambda e, jj0=jj0, W=W: e.dma_start(
            W[:, :, 0:512], wgu_blk[jj0 // 4]),
            reads=[R("cv_wgu")], writes=[rg], war=aclaim(ar, rg))
        p.dma_r("gp", "dwg", 4, lambda e, jj0=jj0, W=W: e.dma_start(
            W[:, :, 512:1024], wgu_blk[16 + jj0 // 4]),
            reads=[R("cv_wgu")], writes=[ru], war=aclaim(ar, ru))

    def attn_down_loader(s, jj_abs, rw):
        p.dma_r("sp", "dwg", 4, lambda e, s=s, jj_abs=jj_abs: e.dma_start(
            INW8[:, s, :], wd_bf[jj_abs * 128:(jj_abs + 1) * 128, :]),
            reads=[R("cv_wd")], writes=[rw], war=aclaim(arINB, rw))

    def q_gate_loader(jj0, W, ar, rg, ru):
        p.dma_r("gp", "dwg", 4, lambda e, jj0=jj0, W=W: e.dma_start(
            W[:, :, 0:512], dqgT[jj0 // 4]),
            reads=[rdq["g"]], writes=[rg], war=aclaim(ar, rg))
        p.dma_r("gp", "dwg", 4, lambda e, jj0=jj0, W=W: e.dma_start(
            W[:, :, 512:1024], dquT[jj0 // 4]),
            reads=[rdq["u"]], writes=[ru], war=aclaim(ar, ru))

    def q_down_loader(s, jj_abs, rw):
        p.dma_r("sp", "dwg", 4, lambda e, s=s, jj_abs=jj_abs: e.dma_start(
            INW8[:, s, :], dqdT[jj_abs]),
            reads=[rdq["d"]], writes=[rw], war=aclaim(arINB, rw))

    for ch2 in range(NC2):
        t0 = ch2 * T2
        aswitch(arCDB)
        aswitch(arINB)
        p.dma_r("sp", "dld", 4, lambda e, t0=t0: e.dma_start(
            CDB[:, :, :], cdv[:, :, t0:t0 + T2]),
            reads=[R("cds")], writes=[rCDB], war=aclaim(arCDB, rCDB))
        p.dma_r("sp", "dld", 4, lambda e, t0=t0: e.dma_start(
            INB[:, :, :], inv[:, :, t0:t0 + T2]),
            reads=[R("ins")], writes=[rINB], war=aclaim(arINB, rINB))
        p.dma_r("sp", "dld", 4, lambda e, t0=t0: e.dma_start(
            XA[:, :, :], x0v[:, :, t0:t0 + T2]),
            reads=[R("x0s")], writes=[rXA], war=aclaim(arXA, rXA))
        rCS = [R(f"cs_{j}") for j in range(DT)]
        rCS0 = R("cs_c0")
        p.emit("dv", lambda e: e.tensor_copy(CS2[:, :, 0], carry[:, :]),
               reads=[R("carry")], writes=[rCS0])
        for j in range(DT):
            p.emit("dv", lambda e, j=j: e.tensor_tensor_scan(
                CS2[:, j, 1:CHN2 + 1], CDB[:, j, CH - 1::CH],
                INB[:, j, CH - 1::CH], carry[:, j:j + 1],
                OP.mult, OP.add),
                reads=[rCDB, rINB, R("carry")], writes=[rCS[j]])
        p.emit("dv", lambda e: e.tensor_tensor(
            YA[:].rearrange("p a (b c) -> p a b c", c=CH),
            CDB[:].rearrange("p a (b c) -> p a b c", c=CH),
            CS2[:, :, 0:CHN2].unsqueeze(3).broadcast_to(
                [128, DT, CHN2, CH]), OP.mult),
            reads=[rCDB, rCS0] + rCS, writes=[rYA])
        p.emit("gp", lambda e: e.tensor_tensor(
            HB[:, :, :], YA[:, :, :], INB[:, :, :], OP.add),
            reads=[rYA, rINB], writes=[rHB])
        p.emit("dv", lambda e: e.tensor_copy(carry[:, :], CS2[:, :, CHN2]),
               reads=rCS, writes=[R("carry")])
        p.dma_r("sp", "dld", 4, lambda e, t0=t0: e.dma_start(
            row(0, T2), c1r_scr[0:1, t0:t0 + T2]),
            reads=[R("c1rs")], writes=[R("row0")])
        p.dma_r("sp", "dld", 4, lambda e, t0=t0: e.dma_start(
            row(512, T2), m1r_scr[0:1, t0:t0 + T2]),
            reads=[R("m1rs")], writes=[R("row0m")])
        sbc, sbr = bcast(row(0, T2), R("row0"), 6, T2)
        msbc, msbr = bcast(row(512, T2), R("row0m"), 7, T2)
        p.emit("dv", lambda e: e.tensor_tensor(
            YA[:, :, :], XA[:, :, :],
            sbc.unsqueeze(1).broadcast_to([128, DT, T2]), OP.mult),
            reads=[rXA] + sbr, writes=[rYA])
        p.emit("dv", lambda e: e.tensor_tensor(
            YA[:, :, :], YA[:, :, :],
            msbc.unsqueeze(1).broadcast_to([128, DT, T2]), OP.subtract),
            reads=[rYA] + msbr, writes=[rYA])
        aswitch(arCDB)
        for j in range(DT):
            s = j % 8
            rw = R(f"ws{s}")
            p.dma_r("gp", "dwa", 4, lambda e, j=j, s=s: e.dma_start(
                wslot(s), op_blk[j]),
                reads=[R("cv_op")], writes=[rw], war=aclaim(arCDB, rw))
            pairs = [(wslot(s)[:, kc, :], HB[:, kc, :]) for kc in range(DT)]
            mmgrp(bank(j % 2), pairs, [PSB[j % 2]], [rw, rHB])
            p.emit("dv", lambda e, j=j: e.tensor_tensor(
                YA[:, j, :], YA[:, j, :], bank(j % 2), OP.add),
                reads=[PSB[j % 2]], writes=[rYA])
        ln_stats(YA, rYA, T2, "rms", 1e-6, 1024)
        r2bc, r2br = bcast(row(1024, T2), R("row1024"), 6, T2)
        for i in range(DT):
            p.emit("dv", lambda e, i=i: e.scalar_tensor_tensor(
                HB[:, i, :], YA[:, i, :], wffn[:, i:i + 1], r2bc,
                OP.mult, OP.mult),
                reads=[rYA, R("wffn")] + r2br, writes=[rHB])
        mlp(rHB, AF.Silu, NFF_A, attn_gate_loader, attn_down_loader, False)
        aswitch(arXA)
        p.dma_r("sp", "dld", 4, lambda e, t0=t0: e.dma_start(
            XA[:, :, :], x0v[:, :, t0:t0 + T2]),
            reads=[R("x0s")], writes=[rXA], war=aclaim(arXA, rXA))
        ln_stats(YA, rYA, T2, "ln", 1e-5, 0)
        l3c, l3cr = bcast(row(0, T2), R("row0"), 6, T2)
        l3m, l3mr = bcast(row(T2, T2), R("row0m"), 7, T2)
        aswitch(arCDB)
        rT3 = R("tmpCDB")
        p.emit("dv", lambda e: e.tensor_tensor(
            CDB[:, :, :], YA[:, :, :],
            l3c.unsqueeze(1).broadcast_to([128, DT, T2]), OP.mult),
            reads=[rYA] + l3cr, writes=[rT3], war=aclaim(arCDB, rT3))
        p.emit("dv", lambda e: e.tensor_tensor(
            CDB[:, :, :], CDB[:, :, :],
            l3m.unsqueeze(1).broadcast_to([128, DT, T2]), OP.subtract),
            reads=[rT3] + l3mr, writes=[rT3])
        p.emit("dv", lambda e: e.tensor_tensor(
            XA[:, :, :], XA[:, :, :], CDB[:, :, :], OP.add),
            reads=[rT3], writes=[rXA])
        ln_stats(XA, rXA, T2, "ln", 1e-5, 1024)
        l4c, l4cr = bcast(row(1024, T2), R("row1024"), 6, T2)
        l4m, l4mr = bcast(row(1024 + T2, T2), R("row1024m"), 7, T2)
        aswitch(arINB)
        rT4 = R("tmpINB")
        p.emit("dv", lambda e: e.tensor_tensor(
            INB[:, :, :], XA[:, :, :],
            l4c.unsqueeze(1).broadcast_to([128, DT, T2]), OP.mult),
            reads=[rXA] + l4cr, writes=[rT4], war=aclaim(arINB, rT4))
        p.emit("dv", lambda e: e.tensor_tensor(
            HB[:, :, :], INB[:, :, :],
            l4m.unsqueeze(1).broadcast_to([128, DT, T2]), OP.subtract),
            reads=[rT4] + l4mr, writes=[rHB])
        p.dma_r("sp", "dsc", 4, lambda e, t0=t0: e.dma_start(
            x1v[:, :, t0:t0 + T2], XA[:, :, :]),
            reads=[rXA], writes=[R("x1s")])
        p.dma_r("sp", "dsc", 4, lambda e, t0=t0: e.dma_start(
            h2v[:, :, t0:t0 + T2], HB[:, :, :]),
            reads=[rHB], writes=[R("h2s")])

    run_dq(10 ** 9)
    for ch2 in range(NC2):
        t0 = ch2 * T2
        p.dma_r("sp", "dld", 4, lambda e, t0=t0: e.dma_start(
            HB[:, :, :], h2v[:, :, t0:t0 + T2]),
            reads=[R("h2s")], writes=[rHB])
        mlp(rHB, AF.Gelu_apprx_tanh, NFF_Q, q_gate_loader, q_down_loader, True)
        aswitch(arXA)
        p.dma_r("sp", "dld", 4, lambda e, t0=t0: e.dma_start(
            XA[:, :, :], x1v[:, :, t0:t0 + T2]),
            reads=[R("x1s")], writes=[rXA], war=aclaim(arXA, rXA))
        ln_stats(YA, rYA, T2, "ln", 1e-5, 0)
        l5c, l5cr = bcast(row(0, T2), R("row0"), 6, T2)
        l5m, l5mr = bcast(row(T2, T2), R("row0m"), 7, T2)
        aswitch(arCDB)
        rT5 = R("tmpCDB")
        p.emit("dv", lambda e: e.tensor_tensor(
            CDB[:, :, :], YA[:, :, :],
            l5c.unsqueeze(1).broadcast_to([128, DT, T2]), OP.mult),
            reads=[rYA] + l5cr, writes=[rT5], war=aclaim(arCDB, rT5))
        p.emit("dv", lambda e: e.tensor_tensor(
            CDB[:, :, :], CDB[:, :, :],
            l5m.unsqueeze(1).broadcast_to([128, DT, T2]), OP.subtract),
            reads=[rT5] + l5mr, writes=[rT5])
        p.emit("dv", lambda e: e.tensor_tensor(
            XA[:, :, :], XA[:, :, :], CDB[:, :, :], OP.add),
            reads=[rT5], writes=[rXA])
        for tb in range(T2 // 128):
            slot = tb % 2
            for bq in range(4):
                def tfn(e, tb=tb, bq=bq):
                    mm = None
                    for k in range(4):
                        dt_i = bq * 4 + k
                        mm = e.transpose(
                            psum[:, bq * 512 + 128 * k:bq * 512 + 128 * (k + 1)],
                            XA[:, dt_i, tb * 128:(tb + 1) * 128], ident[:])
                    return mm
                p.emit("pe", tfn, reads=[rXA, R("ident")], writes=[PSB[bq]])
                p.emit("act", lambda e, bq=bq, slot=slot: e.activation(
                    XT[:, slot, bq * 512:(bq + 1) * 512],
                    psum[:, bq * 512:(bq + 1) * 512], AF.Copy),
                    reads=[PSB[bq]], writes=[rXT[slot]])
            sl = slice(t0 + tb * 128, t0 + (tb + 1) * 128)
            p.dma_r("sp", "dou", 2, lambda e, sl=sl, slot=slot: e.dma_start(
                y_d[sl, :], XT[:, slot, :]),
                reads=[rXT[slot]], writes=[R("yout")])
    p.wait_all("sp")

    p.replay(nc)
    stack.close()
    return nc


_NC_CACHE = {}


def kernel(L0_attn_x, L0_attn_pre_norm_w, L0_attn_w_a, L0_attn_w_i,
           L0_attn_w_v, L0_attn_out_proj_w, L0_attn_decay_bias,
           L0_attn_ffn_norm_w, L0_attn_w_gate_up, L0_attn_w_down,
           L0_ffn_gate_W_packed, L0_ffn_gate_scales, L0_ffn_up_W_packed,
           L0_ffn_up_scales, L0_ffn_down_W_packed, L0_ffn_down_scales):
    x = np.asarray(L0_attn_x, dtype=np.float32)
    B, S, d = x.shape
    assert d == D
    n_cores = 8
    per_b = n_cores // B
    S4 = S // per_b

    if S4 not in _NC_CACHE:
        _NC_CACHE[S4] = build_program(S4)
    nc = _NC_CACHE[S4]

    def as32(a):
        return np.ascontiguousarray(np.asarray(a), dtype=np.float32)

    def col_tiles(v):
        return np.ascontiguousarray(as32(v).reshape(DT, 128).T)

    common = {
        "w_a": as32(L0_attn_w_a), "w_i": as32(L0_attn_w_i),
        "w_v": as32(L0_attn_w_v), "out_proj_w": as32(L0_attn_out_proj_w),
        "w_gate_up": as32(L0_attn_w_gate_up), "w_down": as32(L0_attn_w_down),
        "gate_W": np.ascontiguousarray(np.asarray(L0_ffn_gate_W_packed,
                                                  dtype=np.int32)),
        "gate_S": as32(L0_ffn_gate_scales),
        "up_W": np.ascontiguousarray(np.asarray(L0_ffn_up_W_packed,
                                                dtype=np.int32)),
        "up_S": as32(L0_ffn_up_scales),
        "down_W": np.ascontiguousarray(np.asarray(L0_ffn_down_W_packed,
                                                  dtype=np.int32)),
        "down_S": as32(L0_ffn_down_scales),
        "dbias": col_tiles(L0_attn_decay_bias),
        "wpre": col_tiles(L0_attn_pre_norm_w),
        "wffn": col_tiles(L0_attn_ffn_norm_w),
        "ident": np.eye(128, dtype=np.float32),
    }
    in_maps = []
    for c in range(n_cores):
        b = c // per_b
        pos = c % per_b
        m = dict(common)
        m["x"] = np.ascontiguousarray(x[b, pos * S4:(pos + 1) * S4, :])
        mk = np.zeros((128, 4), dtype=np.float32)
        mk[:, :pos] = 1.0
        m["mask"] = mk
        in_maps.append(m)

    res = run_bass_kernel_spmd(nc, in_maps, core_ids=list(range(n_cores)))
    out = np.empty((B, S, D), dtype=np.float32)
    for c in range(n_cores):
        b = c // per_b
        pos = c % per_b
        out[b, pos * S4:(pos + 1) * S4, :] = res.results[c]["y"]
    return out


if __name__ == "__main__":
    nc = build_program(4096)
    print("built OK")
